# version 09
# baseline (speedup 1.0000x reference)
import numpy as np

import concourse.bass as bass
import concourse.mybir as mybir
from concourse.bass_utils import run_bass_kernel_spmd

F32 = mybir.dt.float32
BF16 = mybir.dt.bfloat16
U8 = mybir.dt.uint8
AF = mybir.ActivationFunctionType
ALU = mybir.AluOpType
AX = mybir.AxisListType

T = 1024
NT = 8
D = 4096
DC = 32
W3 = 3072
W3C = 24
WDC = 8
HD = 16
PAIRS = 8
INT = 8192
INTC = 64
LAT = 2048
LATC = 16
NE = 8
EPS = 1e-5
NEG = -1e9

POOL_B = 207 * 1024
C2_MODE = 3

CO_ID_BF = 0
CO_ID_F32 = 256
CO_MASK = 768
CO_ONES_BF = 1280
CO_ONES_F32 = 1536
CO_CONVBT = 2048
CO_BQT = 2176
CO_BKT = 2208
CO_BOT = 2240
CO_PROJBT = 2272
CO_BV_ROW = 2400
CO_BVH = 6496
CO_BVL = 8544
CO_STATS = 10592
CO_EPS = 10848
CO_SMALLROWS = 10880
CO_BQ64 = 11392
CO_BK64 = 11456
CO_END = 11520

R_T1T = 12288
R_CFM = 77952
RA_XIN0 = 77952
RA_XIN1 = 94336
RA_T1S0 = 110720
RA_T1S1 = 118912
R_CONVSL = 127104
N_CONVSL = 5
R_WQ = 168064
R_WK = 184448

RC_WV = 12288
RC_WO = 28672
RC_VH = 45056
RC_P32B = 61440
RC_PEXPB = 69632
RC_PTSB = 73728

RC_QH64 = 94336
RC_KH64 = 127104
RC_ATTNW = 12288
RC_OALL = 77952

RE_AFM = 110720
RE_PROJSL = 176256
RF_TRASH = 176256
RA_TRASH = 168064
N_PROJSL = 4

RF_XIN0 = 12288
RF_XIN1 = 28672
RF_X2T0 = 45056
RF_X2T1 = 61440
RF_ATM0 = 77952
RF_ATM1 = 86144
RF_T2S0 = 94336
RF_T2S1 = 102528

RG_GB_G = 12288
RG_GB_B = 20480
RG_T2TH = 28672
RG_LAT = 61440
RG_MOE = 77824
RG_HUP = 94208
RG_STM = 159744
RG_SCH0 = 192512
RG_SCH1 = 193536
RG_LOGI0 = 194560
RG_LOGI1 = 194592
RG_MSK8 = 194624
RG_RWROW = 194688
RG_RWB = 196736
RG_GATE = 198784
RG_RSC0 = 199296
RG_RSC1 = 201344
RG_UPSL = 61440
RG_FC1SL = 94208
RG_EXPSL = 110592
RG_DWSL = 28672
RG_F2SL = 61440
RG_FT = 126976
RG_TRASH = 143360
RG_GROW = 126976
RG_BROW = 143360
RG_X2PC0 = 77824
RG_X2PC1 = 86016
RG_GPC0 = 61440
RG_GPC1 = 69632
RG_UPSL3 = 203392

assert RG_UPSL3 + 8192 <= POOL_B
assert RG_RSC1 + 2048 <= POOL_B
assert R_WK + 16384 <= POOL_B


class P:

    def __init__(self, nc):
        self.nc = nc
        self.sems = {}
        self.cnt = {}
        self._auto = set()
        names = ["SA", "SP", "SV", "SC", "SG", "SX0", "SX1", "SS0", "SS1",
                 "WQ", "WK", "WV", "WO"] + [f"D{i}" for i in range(5)]
        for name in names:
            self.sems[name] = nc.alloc_semaphore(name=name)
            self.cnt[name] = 0

    def inc(self, instr, sem, n=1):
        if isinstance(instr, _Auto):
            return self.cnt[sem]
        instr.then_inc(self.sems[sem], n)
        self.cnt[sem] += n
        return self.cnt[sem]

    def wait(self, eng, sem, val=None):
        v = self.cnt[sem] if val is None else val
        if v > 0:
            eng.wait_ge(self.sems[sem], v)


class _Auto:

    def __init__(self, instr):
        self.instr = instr


class EngWrap:

    def __init__(self, p, eng, sem):
        object.__setattr__(self, "_p", p)
        object.__setattr__(self, "_e", eng)
        object.__setattr__(self, "_s", sem)

    def wait_ge(self, sem, v):
        return self._e.wait_ge(sem, v)

    def __getattr__(self, name):
        fn = getattr(self._e, name)
        p, eng, semname = self._p, self._e, self._s

        def wrapped(*a, **k):
            p.wait(eng, semname)
            instr = fn(*a, **k)
            instr.then_inc(p.sems[semname], 1)
            p.cnt[semname] += 1
            return _Auto(instr)

        return wrapped


def build(taps=(), stop_phase=99):
    nc = bass.Bass("TRN2", target_bir_lowering=False, debug=False)
    p = P(nc)
    te, sy, po = nc.tensor, nc.sync, nc.gpsimd
    ve = EngWrap(p, nc.vector, "SV")
    se = EngWrap(p, nc.scalar, "SC")

    dx = nc.dram_tensor("x", [T, D], F32, kind="ExternalInput").ap()
    dxh = nc.dram_tensor("xhalo", [2, D], F32, kind="ExternalInput").ap()
    dwconv = nc.dram_tensor("wconv", [W3C, 3, 128, DC, 128], F32, kind="ExternalInput").ap()
    dwq = nc.dram_tensor("wq", [WDC, 128, 1024], F32, kind="ExternalInput").ap()
    dwk = nc.dram_tensor("wk", [WDC, 128, 1024], F32, kind="ExternalInput").ap()
    dwv = nc.dram_tensor("wv", [WDC, 128, 1024], F32, kind="ExternalInput").ap()
    dwo = nc.dram_tensor("wo", [PAIRS, 128, 1024], F32, kind="ExternalInput").ap()
    dproj = nc.dram_tensor("projw", [DC, 128, WDC, 128], F32, kind="ExternalInput").ap()
    dgate = nc.dram_tensor("gater", [128, DC, NE], F32, kind="ExternalInput").ap()
    dup = nc.dram_tensor("upw", [INTC, 128, DC, 128], F32, kind="ExternalInput").ap()
    ddown = nc.dram_tensor("downw", [DC, 128, INTC, 128], F32, kind="ExternalInput").ap()
    dfc1 = nc.dram_tensor("fc1w", [LATC, 128, DC, 128], F32, kind="ExternalInput").ap()
    dexp = nc.dram_tensor("expw", [LATC, 128, LATC, 128], F32, kind="ExternalInput").ap()
    dfc2 = nc.dram_tensor("fc2w", [DC, 128, LATC, 128], F32, kind="ExternalInput").ap()
    dconvb = nc.dram_tensor("convb", [W3C, 128], F32, kind="ExternalInput").ap()
    dbq = nc.dram_tensor("bq", [16, 64], F32, kind="ExternalInput").ap()
    dbk = nc.dram_tensor("bk", [16, 64], F32, kind="ExternalInput").ap()
    dbv = nc.dram_tensor("bv", [1, 1024], F32, kind="ExternalInput").ap()
    dbo = nc.dram_tensor("bo", [PAIRS, 128], F32, kind="ExternalInput").ap()
    dprojb = nc.dram_tensor("projb", [DC, 128], F32, kind="ExternalInput").ap()
    dgam = nc.dram_tensor("gamma", [1, D], F32, kind="ExternalInput").ap()
    dbet = nc.dram_tensor("beta", [1, D], F32, kind="ExternalInput").ap()
    didb = nc.dram_tensor("id_bf", [128, 128], BF16, kind="ExternalInput").ap()
    didf = nc.dram_tensor("id_f32", [128, 128], F32, kind="ExternalInput").ap()
    dmask = nc.dram_tensor("mask", [128, 128], F32, kind="ExternalInput").ap()
    dout = nc.dram_tensor("out", [T, D], F32, kind="ExternalOutput").ap()
    dx2sp = nc.dram_tensor("x2sp", [NT, 128, D], F32, kind="Internal").ap()
    dt2sp = nc.dram_tensor("t2sp", [NT, 128, DC, 128], BF16, kind="Internal").ap()
    dtaps = {}
    for tname, shape, dt in taps:
        dtaps[tname] = nc.dram_tensor("tap_" + tname, shape, dt, kind="ExternalOutput").ap()

    pool = nc.alloc_sbuf_tensor("pool", [128, POOL_B], U8).ap()

    def bF(off, cols, parts=128):
        apv = pool[:, off : off + cols * 4].bitcast(F32)
        return apv if parts == 128 else apv[0:parts]

    def bB(off, cols, parts=128):
        apv = pool[:, off : off + cols * 2].bitcast(BF16)
        return apv if parts == 128 else apv[0:parts]

    ps8 = nc.alloc_psum_tensor("ps8", [128, 4096], F32).ap()
    ps8b = ps8.bitcast(BF16)
    pbf = [ps8[:, i * 512 : (i + 1) * 512] for i in range(8)]
    pbb = [ps8b[:, i * 1024 : (i + 1) * 1024] for i in range(8)]

    id_bf = bB(CO_ID_BF, 128)
    id_f32 = bF(CO_ID_F32, 128)
    mask = bF(CO_MASK, 128)
    ones_bf = bB(CO_ONES_BF, 128, parts=1)
    ones_f32 = bF(CO_ONES_F32, 128, parts=1)
    convbT = bF(CO_CONVBT, 24)
    bq64 = bF(CO_BQ64, 16, parts=64)
    bk64 = bF(CO_BK64, 16, parts=64)
    boT = bF(CO_BOT, 8)
    projbT = bF(CO_PROJBT, 32)
    bv_row = bF(CO_BV_ROW, 1024, parts=1)
    bvh = bB(CO_BVH, 1024, parts=1)
    bvl = bB(CO_BVL, 1024, parts=1)
    stats = bF(CO_STATS, 64)
    eps_col = bF(CO_EPS, 1)
    srows = bF(CO_SMALLROWS, 128, parts=32)

    def stat(i):
        return stats[:, i : i + 1]

    nc.const_aps.register(F32, EPS, eps_col)

    p.inc(sy.dma_start(id_bf, didb), "SA", 16)
    p.inc(sy.dma_start(id_f32, didf), "SA", 16)
    p.inc(sy.dma_start(mask, dmask), "SA", 16)
    p.inc(sy.dma_start(bv_row, dbv), "SA", 16)
    p.inc(sy.dma_start(srows[0:24], dconvb), "SA", 16)
    ve.memset(ones_f32, 1.0)
    ve.memset(bB(CO_ONES_BF, 128, parts=1), 1.0)
    ve.memset(eps_col, EPS)
    p.wait(ve, "SA")
    ve.tensor_copy(bvh, bv_row)
    p.inc(ve.tensor_tensor(bvl, bv_row, bvh, ALU.subtract), "SV")

    def small_T(n, dst_f32):
        p.wait(te, "SA")
        p.wait(te, "SV")
        it = te.transpose(pbf[0][:, 0:n], srows[0:n], id_f32[0:n, 0:n])
        c = p.inc(it, "SP")
        p.wait(ve, "SP", c)
        p.inc(ve.tensor_copy(dst_f32, pbf[0][:, 0:n]), "SV")
        return c

    c_st = small_T(24, convbT)
    for dsrc, n, dst in ((dbo, 8, boT), (dprojb, 32, projbT)):
        p.wait(sy, "SP", c_st)
        p.inc(sy.dma_start(srows[0:n], dsrc), "SA", 16)
        c_st = small_T(n, dst)
    for dsrc, dst in ((dbq, bq64), (dbk, bk64)):
        p.wait(sy, "SP", c_st)
        p.inc(sy.dma_start(srows[0:16, 0:64], dsrc), "SA", 16)
        p.wait(te, "SA")
        p.wait(te, "SV")
        it = te.transpose(pbf[0][0:64, 0:16], srows[0:16, 0:64], id_f32[0:16, 0:16])
        c_st = p.inc(it, "SP")
        p.wait(ve, "SP", c_st)
        p.inc(ve.tensor_copy(dst, pbf[0][0:64, 0:16]), "SV")

    S_SUM, S_SQ, S_MU, S_VAR, S_R, S_NEGM, S_L, S_M1, S_M2, S_RW = range(10)

    def finish():
        for s in ("SA", "SX0", "SX1", "SS0", "SS1", "WQ", "WK", "WV", "WO",
                  "D0", "D1", "D2", "D3", "D4"):
            p.wait(sy, s)
        return nc

    def tap_barrier():
        if dtaps:
            for q in (te, sy, po, nc.vector, nc.scalar):
                p.wait(q, "SA")

    def ln_stats_fast(src, sq_trash, sum_trash, width=D):
        iv = ve.tensor_scalar(sum_trash, src, 1.0, 0.0, ALU.mult, ALU.add,
                              accum_out=stat(S_SUM))
        p.inc(iv, "SV")
        ia = se.activation(sq_trash, src, AF.Square, accum_out=stat(S_SQ))
        c_sq = p.inc(ia, "SC")
        ve.tensor_scalar(stat(S_MU), stat(S_SUM), 1.0 / width, None, ALU.mult)
        p.wait(ve, "SC", c_sq)
        ve.tensor_tensor(stat(S_VAR), stat(S_MU), stat(S_MU), ALU.mult)
        iv = ve.tensor_scalar(stat(S_VAR), stat(S_SQ), 1.0 / width, stat(S_VAR),
                              ALU.mult, ALU.subtract)
        c_v = p.inc(iv, "SV")
        p.wait(se, "SV", c_v)
        se.activation(stat(S_R), stat(S_VAR), AF.Ln, bias=EPS, scale=1.0)
        return p.inc(se.activation(stat(S_R), stat(S_R), AF.Exp, bias=0.0, scale=-0.5), "SC")

    def ln_stats(src, sq_trash, width=D):
        p.inc(ve.reduce_sum(stat(S_SUM), src, axis=AX.X), "SV")
        ia = se.activation(sq_trash, src, AF.Square, accum_out=stat(S_SQ))
        c_sq = p.inc(ia, "SC")
        ve.tensor_scalar(stat(S_MU), stat(S_SUM), 1.0 / width, None, ALU.mult)
        p.wait(ve, "SC", c_sq)
        ve.tensor_tensor(stat(S_VAR), stat(S_MU), stat(S_MU), ALU.mult)
        iv = ve.tensor_scalar(stat(S_VAR), stat(S_SQ), 1.0 / width, stat(S_VAR),
                              ALU.mult, ALU.subtract)
        c_v = p.inc(iv, "SV")
        p.wait(se, "SV", c_v)
        se.activation(stat(S_R), stat(S_VAR), AF.Ln, bias=EPS, scale=1.0)
        return p.inc(se.activation(stat(S_R), stat(S_R), AF.Exp, bias=0.0, scale=-0.5), "SC")

    t1T = bB(R_T1T, DC * 1026)
    t1T3 = t1T.rearrange("p (k c) -> p k c", k=DC)
    trashA = bF(RA_TRASH, D)
    xin = [bF(RA_XIN0, D), bF(RA_XIN1, D)]
    t1s = [bB(RA_T1S0, D), bB(RA_T1S1, D)]
    xin_free = [0, 0]
    t1s_free = [0, 0]
    tb_free = [0, 0]

    def tm_to_fm(src_bf, dst_groups):
        for g in range(8):
            tb = 4 + (g % 2)
            p.wait(te, "SV", tb_free[g % 2])
            for j in range(4):
                ch = g * 4 + j
                it = te.transpose(pbb[tb][:, j * 128 : (j + 1) * 128],
                                  src_bf[:, ch * 128 : (ch + 1) * 128], id_bf)
            c = p.inc(it, "SP")
            p.wait(ve, "SP", c)
            iv = ve.tensor_copy(dst_groups[g],
                                pbb[tb].rearrange("p (a b) -> p a b", a=8)[:, 0:4])
            tb_free[g % 2] = p.inc(iv, "SV")

    for t in range(NT + 1):
        par = t % 2
        if t == NT:
            p.wait(ve, "SV", xin_free[par])
            c = p.inc(ve.memset(xin[par], 0.0), "SV")
            p.wait(sy, "SV", c)
            ins = sy.dma_start(xin[par][0:2], dxh)
        else:
            p.wait(sy, "SV", xin_free[par])
            ins = sy.dma_start(xin[par], dx[t * 128 : (t + 1) * 128])
        c_load = p.inc(ins, f"SX{par}", 16)
        p.wait(ve, f"SX{par}", c_load)
        p.wait(se, f"SX{par}", c_load)
        p.wait(se, "SP", t1s_free[par])
        c_r = ln_stats_fast(xin[par], t1s[par], trashA)
        p.wait(ve, "SC", c_r)
        iv = ve.tensor_scalar(t1s[par], xin[par], stat(S_MU), stat(S_R),
                              ALU.subtract, ALU.mult)
        c_n = p.inc(iv, "SV")
        xin_free[par] = c_n
        p.wait(te, "SV", c_n)
        if t < NT:
            tm_to_fm(t1s[par],
                     [t1T3[:, g * 4 : g * 4 + 4, 1 + t * 128 : 1 + (t + 1) * 128]
                      for g in range(8)])
        else:
            for g in range(8):
                tb = 4 + (g % 2)
                p.wait(te, "SV", tb_free[g % 2])
                for j in range(4):
                    ch = g * 4 + j
                    it = te.transpose(pbb[tb][:, j * 128 : (j + 1) * 128],
                                      t1s[par][:, ch * 128 : (ch + 1) * 128], id_bf)
                c = p.inc(it, "SP")
                p.wait(ve, "SP", c)
                src3 = pbb[tb].rearrange("p (a b) -> p a b", a=8)
                ve.tensor_copy(t1T3[:, g * 4 : g * 4 + 4, 0:1], src3[:, 0:4, 0:1])
                iv = ve.tensor_copy(t1T3[:, g * 4 : g * 4 + 4, 1025:1026], src3[:, 0:4, 1:2])
                tb_free[g % 2] = p.inc(iv, "SV")
        t1s_free[par] = p.cnt["SP"]
    t1T_ready = p.cnt["SV"]

    if "t1T" in dtaps:
        p.wait(sy, "SA")
        p.wait(sy, "SV", t1T_ready)
        p.inc(sy.dma_start(dtaps["t1T"], t1T), "SA", 16)
    if stop_phase <= 0:
        return finish()

    c_fm = bB(R_CFM, W3C * 1024)
    conv_sl = [bB(R_CONVSL + s * 8192, DC * 128) for s in range(N_CONVSL)]
    wq_sb = bB(R_WQ, WDC * 1024)
    wk_sb = bB(R_WK, WDC * 1024)
    slot_last = [0] * N_CONVSL
    group_ready = {}
    bank_free = {i: ("SC", 0) for i in range(4)}

    p.wait(te, "SV", t1T_ready)
    gi = 0
    for m in range(W3C):
        for dw in range(3):
            s = gi % N_CONVSL
            p.wait(po, "SP", slot_last[s])
            ins = po.dma_start(conv_sl[s], dwconv[m, dw].rearrange("p a b -> p (a b)"))
            group_ready[gi] = (f"D{s}", p.inc(ins, f"D{s}", 16))
            gi += 1
        for half in range(2):
            bk = (m * 2 + half) % 4
            sem, v = bank_free[bk]
            p.wait(te, sem, v)
            for dw in range(3):
                g = m * 3 + dw
                p.wait(te, *group_ready[g])
                sl = conv_sl[g % N_CONVSL]
                base = half * 512 + dw
                for k in range(DC):
                    it = te.matmul(pbf[bk], sl[:, k * 128 : (k + 1) * 128],
                                   t1T[:, k * 1026 + base : k * 1026 + base + 512],
                                   start=(dw == 0 and k == 0),
                                   stop=(dw == 2 and k == DC - 1))
            c_mm = p.inc(it, "SP")
            if half == 1:
                for dw in range(3):
                    slot_last[(m * 3 + dw) % N_CONVSL] = c_mm
            p.wait(se, "SP", c_mm)
            ia = se.activation(c_fm[:, m * 1024 + half * 512 : m * 1024 + half * 512 + 512],
                               pbf[bk], AF.Identity, bias=convbT[:, m : m + 1], scale=1.0)
            bank_free[bk] = ("SC", p.inc(ia, "SC"))
    c_ready = p.cnt["SC"]
    p.inc(po.dma_start(wq_sb.rearrange("p (k c) -> p k c", k=WDC), dwq.rearrange("k p c -> p k c")), "WQ", 16)
    p.inc(po.dma_start(wk_sb.rearrange("p (k c) -> p k c", k=WDC), dwk.rearrange("k p c -> p k c")), "WK", 16)


    if "c" in dtaps:
        p.wait(sy, "SA")
        p.wait(sy, "SC", c_ready)
        p.inc(sy.dma_start(dtaps["c"], c_fm), "SA", 16)
    if stop_phase <= 1:
        return finish()

    tap_barrier()
    wv_sb = bB(RC_WV, WDC * 1024)
    wo_sb = bB(RC_WO, PAIRS * 1024)
    vh_sb = bB(RC_VH, NT * 1024)
    o_all = bB(RC_OALL, PAIRS * 1024)
    qh64 = bB(RC_QH64, HD * 1024, parts=64)
    kh64 = bB(RC_KH64, HD * 1024, parts=64)
    attnW = bB(RC_ATTNW, PAIRS * 1024)
    p32b = bF(RC_P32B, 2048)
    pexpb = bB(RC_PEXPB, 2048)
    ptsb = bB(RC_PTSB, 2048)
    negm16 = stats[:, 16:32]
    linv16 = stats[:, 32:48]
    p32h = p32b.rearrange("p (h c) -> p h c", h=HD)
    pexph = pexpb.rearrange("p (h c) -> p h c", h=HD)
    mask_b = mask.unsqueeze(1).broadcast_to([128, HD, 128])
    ps_sc = ps8[:, 1024:3072]

    p.wait(po, "SP")
    p.inc(po.dma_start(wv_sb.rearrange("p (k c) -> p k c", k=WDC), dwv.rearrange("k p c -> p k c")), "WV", 16)
    p.inc(po.dma_start(wo_sb.rearrange("p (k c) -> p k c", k=PAIRS), dwo.rearrange("k p c -> p k c")), "WO", 16)

    fb = {0: bank_free[0], 1: bank_free[1]}
    fb_next = [0]

    def fill_bank():
        bk = fb_next[0]
        fb_next[0] ^= 1
        sem, v = fb[bk]
        p.wait(te, sem, v)
        return bk

    first_flags = {"qk": True, "vh": True, "wo": True}

    def u_qk(which, pr, half):
        if first_flags["qk"]:
            p.wait(te, "WQ")
            p.wait(te, "WK")
            p.wait(te, "SC", c_ready)
            first_flags["qk"] = False
        w_sb = wq_sb if which == 0 else wk_sb
        b64 = bq64 if which == 0 else bk64
        dst = qh64 if which == 0 else kh64
        bk = fill_bank()
        for k in range(WDC):
            it = te.matmul(pbf[bk], w_sb[:, k * 1024 + pr * 128 : k * 1024 + (pr + 1) * 128],
                           c_fm[:, (which * 8 + k) * 1024 + half * 512
                                : (which * 8 + k) * 1024 + half * 512 + 512],
                           start=(k == 0), stop=(k == WDC - 1))
        c_mm = p.inc(it, "SP")
        p.wait(se, "SP", c_mm)
        for j in range(2):
            h = 2 * pr + j
            ia = se.activation(dst[:, h * 1024 + half * 512 : h * 1024 + half * 512 + 512],
                               pbf[bk][j * 64 : (j + 1) * 64, 0:512], AF.Identity,
                               bias=b64[:, h : h + 1], scale=1.0)
        fb[bk] = ("SC", p.inc(ia, "SC"))

    vh_done = {}

    def u_vh(b, hv):
        if first_flags["vh"]:
            p.wait(te, "WV")
            first_flags["vh"] = False
        bk = fill_bank()
        for k in range(WDC):
            te.matmul(pbf[bk],
                      c_fm[:, (16 + k) * 1024 + b * 128 : (16 + k) * 1024 + (b + 1) * 128],
                      wv_sb[:, k * 1024 + hv * 512 : k * 1024 + hv * 512 + 512],
                      start=(k == 0), stop=False)
        te.matmul(pbf[bk], ones_bf, bvh[:, hv * 512 : hv * 512 + 512],
                  start=False, stop=False)
        it = te.matmul(pbf[bk], ones_bf, bvl[:, hv * 512 : hv * 512 + 512],
                       start=False, stop=True)
        c_mm = p.inc(it, "SP")
        p.wait(ve, "SP", c_mm)
        iv = ve.tensor_copy(vh_sb[:, b * 1024 + hv * 512 : b * 1024 + hv * 512 + 512],
                            pbf[bk])
        fb[bk] = ("SV", p.inc(iv, "SV"))
        if hv == 1:
            vh_done[b] = p.cnt["SV"]

    def u_wo(m, half, oev_ready):
        if first_flags["wo"]:
            p.wait(te, "WO")
            first_flags["wo"] = False
        bk = fill_bank()
        p.wait(te, "SV", oev_ready)
        for pr in range(PAIRS):
            it = te.matmul(pbf[bk],
                           wo_sb[:, pr * 1024 + m * 128 : pr * 1024 + (m + 1) * 128],
                           o_all[:, pr * 1024 + half * 512 : pr * 1024 + half * 512 + 512],
                           start=(pr == 0), stop=(pr == PAIRS - 1))
        c_mm = p.inc(it, "SP")
        p.wait(se, "SP", c_mm)
        ia = se.activation(attnW[:, m * 1024 + half * 512 : m * 1024 + half * 512 + 512],
                           pbf[bk], AF.Identity, bias=boT[:, m : m + 1], scale=1.0)
        fb[bk] = ("SC", p.inc(ia, "SC"))

    sc_mm, tr_mm = {}, {}
    stt_done, norm_done, pt_done, oev_last = {}, {}, {}, {}
    obank = {}
    qk_ready = {}

    def attn_scores(b):
        p.wait(te, "SV", stt_done.get(b - 1, 0))
        p.wait(te, "SC", qk_ready[b // 4])
        for h in range(HD):
            it = te.matmul(ps_sc[:, h * 128 : (h + 1) * 128],
                           qh64[:, h * 1024 + b * 128 : h * 1024 + (b + 1) * 128],
                           kh64[:, h * 1024 + b * 128 : h * 1024 + (b + 1) * 128],
                           start=True, stop=True)
        sc_mm[b] = p.inc(it, "SP")

    def attn_softmax(b):
        p.wait(ve, "SP", sc_mm[b])
        iv = ve.scalar_tensor_tensor(p32b, ps_sc, 0.125, mask_b, ALU.mult, ALU.add)
        stt_done[b] = p.inc(iv, "SV")
        ve.reduce_max(negm16, p32h, axis=AX.X, negate=True)
        ve.tensor_tensor(p32h, p32h,
                         negm16.unsqueeze(2).broadcast_to([128, HD, 128]), ALU.add)
        c_sub = p.cnt["SV"]
        p.wait(se, "SV", c_sub)
        p.wait(se, "SP", tr_mm.get(b - 1, 0))
        ia = se.activation(pexpb, p32b, AF.Exp)
        c_e = p.inc(ia, "SC")
        p.wait(ve, "SC", c_e)
        ve.reduce_sum(linv16, pexph, axis=AX.X)
        ve.reciprocal(linv16, linv16)
        iv = ve.tensor_tensor(pexph, pexph,
                              linv16.unsqueeze(2).broadcast_to([128, HD, 128]), ALU.mult)
        norm_done[b] = p.inc(iv, "SV")

    def attn_pt_o(b):
        p.wait(te, "SV", norm_done[b])
        p.wait(te, "SV", oev_last.get(b - 1, 0))
        for h in range(HD):
            it = te.transpose(ps8b[:, 6144 + h * 128 : 6144 + (h + 1) * 128],
                              pexpb[:, h * 128 : (h + 1) * 128], id_bf)
        tr_mm[b] = p.inc(it, "SP")
        p.wait(ve, "SP", tr_mm[b])
        iv = ve.tensor_copy(ptsb, ps8b[:, 6144:8192])
        pt_done[b] = p.inc(iv, "SV")
        p.wait(te, "SV", pt_done[b])
        p.wait(te, "SV", vh_done[b])
        for pr in range(PAIRS):
            bk6 = 6 + (pr % 2)
            p.wait(te, "SV", obank.get(bk6, 0))
            for j in range(2):
                h = 2 * pr + j
                it = te.matmul(pbf[bk6][j * 64 : (j + 1) * 64, 0:128],
                               vh_sb[:, b * 1024 + h * 64 : b * 1024 + (h + 1) * 64],
                               ptsb[:, h * 128 : (h + 1) * 128], start=True, stop=True)
            c_mm = p.inc(it, "SP")
            p.wait(ve, "SP", c_mm)
            iv = ve.tensor_copy(o_all[:, pr * 1024 + b * 128 : pr * 1024 + (b + 1) * 128],
                                pbf[bk6][:, 0:128])
            obank[bk6] = p.inc(iv, "SV")
            oev_last[b] = obank[bk6]

    for pr in range(PAIRS):
        u_qk(1, pr, 0)
        u_qk(1, pr, 1)
    for b in range(NT):
        u_vh(b, 0)
        u_vh(b, 1)
    for pr in range(PAIRS):
        u_qk(0, pr, 0)
        u_qk(0, pr, 1)
    qk_ready[0] = qk_ready[1] = p.cnt["SC"]

    units2 = None
    for b in range(NT):
        if b == 4:
            units2 = [lambda m=m: u_wo(m, 0, oev_last[3]) for m in range(WDC)]
        attn_scores(b)
        attn_softmax(b)
        if b >= 4 and units2:
            for _ in range(2):
                if units2:
                    units2.pop(0)()
        attn_pt_o(b)
    while units2:
        units2.pop(0)()
    o_ready = p.cnt["SV"]

    if "qh" in dtaps:
        p.wait(sy, "SA")
        p.wait(sy, "SC", qk_ready[1])
        p.inc(sy.dma_start(dtaps["qh"], qh64), "SA", 16)
        p.inc(sy.dma_start(dtaps["kh"], kh64), "SA", 16)
        p.wait(sy, "SV", o_ready)
        p.inc(sy.dma_start(dtaps["vh"], vh_sb), "SA", 16)
    if "o" in dtaps:
        p.wait(sy, "SA")
        p.wait(sy, "SV", o_ready)
        p.inc(sy.dma_start(dtaps["o"], o_all), "SA", 16)
    if stop_phase <= 3:
        return finish()

    tap_barrier()
    for m in range(WDC):
        u_wo(m, 1, oev_last[NT - 1])
    wo_done = p.cnt["SC"]

    a_fm = bB(RE_AFM, DC * 1024)
    proj_sl = [bB(RE_PROJSL + s * 2048, WDC * 128) for s in range(N_PROJSL)]
    slot_last = [p.cnt["SP"]] * N_PROJSL
    slot_sem = ["SP"] * N_PROJSL
    group_ready = {}

    def proj_dma(m):
        s = m % N_PROJSL
        p.wait(po, slot_sem[s], slot_last[s])
        ins = po.dma_start(proj_sl[s], dproj[m].rearrange("p a b -> p (a b)"))
        group_ready[m] = (f"D{s}", p.inc(ins, f"D{s}", 16))

    for m in range(N_PROJSL):
        proj_dma(m)
    p.wait(te, "SC", wo_done)
    for m in range(DC):
        s = m % N_PROJSL
        p.wait(te, *group_ready[m])
        for half in range(2):
            bk = fill_bank()
            for k in range(WDC):
                it = te.matmul(pbf[bk], proj_sl[s][:, k * 128 : (k + 1) * 128],
                               attnW[:, k * 1024 + half * 512 : k * 1024 + half * 512 + 512],
                               start=(k == 0), stop=(k == WDC - 1))
            c_mm = p.inc(it, "SP")
            p.wait(se, "SP", c_mm)
            ia = se.activation(a_fm[:, m * 1024 + half * 512 : m * 1024 + half * 512 + 512],
                               pbf[bk], AF.Identity, bias=projbT[:, m : m + 1], scale=1.0)
            fb[bk] = ("SC", p.inc(ia, "SC"))
        slot_last[s] = p.cnt["SP"]
        if m + N_PROJSL < DC:
            proj_dma(m + N_PROJSL)
    a_ready = p.cnt["SC"]
    bank_free = {i: ("SC", a_ready) for i in range(4)}

    if "a" in dtaps:
        p.wait(sy, "SA")
        p.wait(sy, "SC", a_ready)
        p.inc(sy.dma_start(dtaps["a"], a_fm), "SA", 16)
    if stop_phase <= 4:
        return finish()

    tap_barrier()
    xinF = [bF(RF_XIN0, D), bF(RF_XIN1, D)]
    x2t = [bF(RF_X2T0, D), bF(RF_X2T1, D)]
    stage = [bB(RF_X2T0, D), bB(RF_X2T1, D)]
    atm = [bB(RF_ATM0, D), bB(RF_ATM1, D)]
    t2s = [bB(RF_T2S0, D), bB(RF_T2S1, D)]
    trashF = bF(RF_TRASH, D)
    af = [0, 0]
    tb_free2 = [0, 0]
    xin_free = [0, 0]
    x2_spilled = [0, 0]
    t2s_free = [0, 0]
    atm_free = [0, 0]
    p.wait(sy, "SC", a_ready)
    p.wait(te, "SC", a_ready)
    for t in range(NT):
        par = t % 2
        p.wait(sy, "SG", xin_free[par])
        c_load = p.inc(sy.dma_start(xinF[par], dx[t * 128 : (t + 1) * 128]), f"SX{par}", 16)
        atm3 = atm[par].rearrange("p (a b) -> p a b", a=DC)
        p.wait(te, "SG", atm_free[par])
        for g in range(8):
            tb = 4 + (g % 2)
            p.wait(te, "SC", af[g % 2])
            for j in range(4):
                m = g * 4 + j
                it = te.transpose(pbb[tb][:, j * 128 : (j + 1) * 128],
                                  a_fm[:, m * 1024 + t * 128 : m * 1024 + (t + 1) * 128],
                                  id_bf)
            c = p.inc(it, "SP")
            p.wait(se, "SP", c)
            ia = se.activation(atm3[:, g * 4 : g * 4 + 4],
                               pbb[tb].rearrange("p (a b) -> p a b", a=8)[:, 0:4],
                               AF.Identity)
            af[g % 2] = p.inc(ia, "SC")
        p.wait(ve, f"SX{par}", c_load)
        p.wait(se, f"SS{par}", x2_spilled[par])
        p.wait(ve, "SC", af[0])
        p.wait(ve, "SC", af[1])
        c_r = ln_stats_fast(atm[par], x2t[par], trashF)
        p.wait(ve, "SC", c_r)
        p.wait(ve, f"SS{par}", x2_spilled[par])
        iv = ve.tensor_scalar(x2t[par], atm[par], stat(S_MU), stat(S_R),
                              ALU.subtract, ALU.mult)
        c_g = p.inc(iv, "SV")
        p.wait(po, "SV", c_g)
        p.wait(po, f"SX{par}", c_load)
        it = po.tensor_tensor(x2t[par], x2t[par], xinF[par], ALU.add)
        c_x2 = p.inc(it, "SG")
        xin_free[par] = c_x2
        atm_free[par] = c_x2
        p.wait(sy, "SG", c_x2)
        c_spill = p.inc(sy.dma_start(dx2sp[t], x2t[par]), f"SS{par}", 16)
        p.wait(se, "SG", c_x2)
        p.wait(ve, "SG", c_x2)
        p.wait(se, f"SS{par}", t2s_free[par])
        c_r2 = ln_stats_fast(x2t[par], t2s[par], trashF)
        p.wait(ve, "SC", c_r2)
        iv = ve.tensor_scalar(t2s[par], x2t[par], stat(S_MU), stat(S_R),
                              ALU.subtract, ALU.mult)
        c_t2 = p.inc(iv, "SV")
        p.wait(te, "SV", c_t2)
        p.wait(te, f"SS{par}", c_spill)
        st3 = stage[par].rearrange("p (a b) -> p a b", a=DC)
        for g in range(8):
            tb = 2 + (g % 2)
            p.wait(te, "SV", tb_free2[g % 2])
            for j in range(4):
                ch = g * 4 + j
                it = te.transpose(pbb[tb][:, j * 128 : (j + 1) * 128],
                                  t2s[par][:, ch * 128 : (ch + 1) * 128], id_bf)
            c = p.inc(it, "SP")
            p.wait(ve, "SP", c)
            iv = ve.tensor_copy(st3[:, g * 4 : g * 4 + 4],
                                pbb[tb].rearrange("p (a b) -> p a b", a=8)[:, 0:4])
            tb_free2[g % 2] = p.inc(iv, "SV")
        c_stage = p.cnt["SV"]
        p.wait(sy, "SV", c_stage)
        ins = sy.dma_start(dt2sp[t], stage[par].rearrange("p (k c) -> p k c", k=DC))
        t2s_free[par] = p.inc(ins, f"SS{par}", 16)
        x2_spilled[par] = t2s_free[par]
    f_ss = (p.cnt["SS0"], p.cnt["SS1"])
    f_sv = p.cnt["SV"]

    if "x2" in dtaps:
        p.wait(sy, "SA")
        p.wait(sy, "SS0", f_ss[0]); p.wait(sy, "SS1", f_ss[1])
        p.inc(sy.dma_start(dtaps["x2"], dx2sp.rearrange("t p d -> (t p) d")), "SA", 16)
    if "t2sp" in dtaps:
        p.wait(sy, "SS0", f_ss[0]); p.wait(sy, "SS1", f_ss[1])
        p.inc(sy.dma_start(dtaps["t2sp"], dt2sp.rearrange("t p k c -> (t p) (k c)")), "SA", 16)
    if stop_phase <= 5:
        return finish()

    tap_barrier()
    gb_g = bB(RG_GB_G, D)
    gb_b = bB(RG_GB_B, D)
    grow = bF(RG_GROW, D, parts=1)
    brow = bF(RG_BROW, D, parts=1)
    gate_sb = bB(RG_GATE, DC * NE)
    t2Th = bB(RG_T2TH, DC * 512)
    lat = bB(RG_LAT, LATC * 512)
    moe = bB(RG_MOE, LATC * 512)
    hup = bB(RG_HUP, INTC * 512)
    stm = [bB(RG_STM + i * 8192, D) for i in range(4)]
    sch = [bB(RG_SCH0, 512), bB(RG_SCH1, 512)]
    logi = [bF(RG_LOGI0, NE), bF(RG_LOGI1, NE)]
    msk8 = bF(RG_MSK8, NE)
    rwrow = bF(RG_RWROW, 512, parts=1)
    rwb = bF(RG_RWB, 512)
    rsc = [bF(RG_RSC0, 512), bF(RG_RSC1, 512)]
    ftile = bF(RG_FT, D)
    trash = bF(RG_TRASH, D)
    x2pc = [bF(RG_X2PC0, 2048), bF(RG_X2PC1, 2048)]
    gpc = [bF(RG_GPC0, 2048), bF(RG_GPC1, 2048)]
    fc1_sl = [bB(RG_FC1SL + s * 8192, DC * 128) for s in range(2)]
    exp_sl = [bB(RG_EXPSL + s * 8192, 2 * LATC * 128) for s in range(2)]
    up_sl = [bB(RG_UPSL, DC * 128), bB(RG_UPSL + 8192, DC * 128), bB(RG_UPSL3, DC * 128)]
    dw_sl = [bB(RG_DWSL + s * 16384, INTC * 128) for s in range(2)]
    f2_sl = [bB(RG_F2SL + s * 8192, 2 * LATC * 128) for s in range(2)]

    for q in (sy, po):
        p.wait(q, "SS0", f_ss[0])
        p.wait(q, "SS1", f_ss[1])
    p.inc(sy.dma_start(grow, dgam), "SA", 16)
    p.inc(sy.dma_start(brow, dbet), "SA", 16)
    cg = p.inc(po.dma_start(gate_sb, dgate.rearrange("p a b -> p (a b)")), "WQ", 16)
    p.wait(te, "SA")
    bank_free = {i: ("SC", a_ready) for i in range(4)}
    for row, dst in ((grow, gb_g), (brow, gb_b)):
        for g in range(8):
            bk = g % 2
            sem, v = bank_free[bk]
            p.wait(te, sem, v)
            it = te.matmul(pbf[bk], ones_f32, row[:, g * 512 : (g + 1) * 512],
                           start=True, stop=True)
            c_mm = p.inc(it, "SP")
            p.wait(ve, "SP", c_mm)
            iv = ve.tensor_copy(dst[:, g * 512 : (g + 1) * 512], pbf[bk])
            bank_free[bk] = ("SV", p.inc(iv, "SV"))
    gb_ready = p.cnt["SV"]

    t2h3 = t2Th.rearrange("p (k c) -> p k c", k=DC)
    t2_loads = {}

    def load_t2(half):
        p.wait(sy, "SA")
        for tl in range(4):
            ins = sy.dma_start(t2h3[:, :, tl * 128 : (tl + 1) * 128],
                               dt2sp[half * 4 + tl])
            t2_loads[half] = p.inc(ins, "SA", 16)

    load_t2(0)
    p.wait(te, "SC")
    tb_free[0] = tb_free[1] = p.cnt["SV"]
    for half in range(2):
        if half == 1:
            p.wait(po, "SP", down_done)
            if dtaps:
                p.wait(po, "SA")
                p.wait(sy, "SA")

        p.wait(te, "SA", t2_loads[half])
        p.wait(te, "WQ", cg)
        last_rw_t = 0
        for tl in range(4):
            bk = tl % 2
            sem, v = bank_free[bk]
            p.wait(te, sem, v)
            for k in range(DC):
                it = te.matmul(pbf[bk][:, 0:NE],
                               t2Th[:, k * 512 + tl * 128 : k * 512 + (tl + 1) * 128],
                               gate_sb[:, k * NE : (k + 1) * NE],
                               start=(k == 0), stop=(k == DC - 1))
            c_mm = p.inc(it, "SP")
            par = tl % 2
            p.wait(ve, "SP", c_mm)
            iv = ve.tensor_copy(logi[par], pbf[bk][:, 0:NE])
            bank_free[bk] = ("SV", p.inc(iv, "SV"))
            ve.reduce_max(stat(S_M1), logi[par], axis=AX.X)
            ve.tensor_scalar(msk8, logi[par], stat(S_M1), None, ALU.is_ge)
            ve.scalar_tensor_tensor(msk8, msk8, -1e30, logi[par], ALU.mult, ALU.add)
            ve.reduce_max(stat(S_M2), msk8, axis=AX.X)
            iv = ve.tensor_tensor(stat(S_RW), stat(S_M1), stat(S_M2), ALU.add)
            c_rw = p.inc(iv, "SV")
            p.wait(te, "SV", c_rw)
            it = te.transpose(pbf[2][0:1, tl * 128 : (tl + 1) * 128], stat(S_RW), id_f32)
            last_rw_t = p.inc(it, "SP")
        p.wait(ve, "SP", last_rw_t)
        iv = ve.tensor_copy(rwrow, pbf[2][0:1, 0:512])
        c_row = p.inc(iv, "SV")
        p.wait(te, "SV", c_row)
        it = te.matmul(pbf[3], ones_f32, rwrow, start=True, stop=True)
        c_mm = p.inc(it, "SP")
        p.wait(ve, "SP", c_mm)
        rwb_ready = p.inc(ve.tensor_copy(rwb, pbf[3]), "SV")
        bank_free[2] = ("SV", rwb_ready)
        bank_free[3] = ("SV", rwb_ready)

        def p1(n_m, n_k, dram_fn, rhs_fn, evac_fn, slots, slot_init, dsem):
            L = len(slots)
            slot_last = list(slot_init)
            ready = {}

            def dma(m):
                s = m % L
                p.wait(po, "SP", slot_last[s])
                ready[m] = (dsem[s], p.inc(po.dma_start(slots[s], dram_fn(m)), dsem[s], 16))

            for m0 in range(min(L - 1, n_m)):
                dma(m0)
            last_mm = 0
            for m in range(n_m):
                s = m % L
                if m + L - 1 < n_m:
                    dma(m + L - 1)
                bk = m % 2
                sem, v = bank_free[bk]
                p.wait(te, sem, v)
                p.wait(te, *ready[m])
                for k in range(n_k):
                    it = te.matmul(pbf[bk], slots[s][:, k * 128 : (k + 1) * 128],
                                   rhs_fn(k), start=(k == 0), stop=(k == n_k - 1))
                c_mm = p.inc(it, "SP")
                slot_last[s] = c_mm
                last_mm = c_mm
                evac_fn(m, bk, c_mm)
            return last_mm

        fc1_first = [half == 1]

        def fc1_evac(m, bk, c_mm):
            p.wait(ve, "SP", c_mm)
            if fc1_first[0]:
                p.wait(ve, "SS0")
                p.wait(ve, "SS1")
                fc1_first[0] = False
            iv = ve.tensor_copy(lat[:, m * 512 : (m + 1) * 512], pbf[bk])
            bank_free[bk] = ("SV", p.inc(iv, "SV"))

        fc1_done = p1(LATC, DC, lambda m: dfc1[m].rearrange("p a b -> p (a b)"),
                      lambda k: t2Th[:, k * 512 : (k + 1) * 512],
                      fc1_evac, fc1_sl, [0, 0], ("D0", "D1"))

        def exp_evac(m, bk, c_mm):
            p.wait(ve, "SP", c_mm)
            p.wait(ve, "SV", rwb_ready)
            iv = ve.scalar_tensor_tensor(moe[:, m * 512 : (m + 1) * 512],
                                         pbf[bk], 1.0, rwb, ALU.mult, ALU.mult)
            bank_free[bk] = ("SV", p.inc(iv, "SV"))

        p.wait(te, "SV")
        exp_slot_last = [0, 0]
        exp_ready = {}

        def exp_dma(g):
            s = g % 2
            p.wait(po, "SP", exp_slot_last[s])
            ins = po.dma_start(
                exp_sl[s].rearrange("p (g c) -> p g c", g=2),
                dexp[2 * g : 2 * g + 2].rearrange("g p a b -> p g (a b)"))
            exp_ready[g] = (f"D{2 + s}", p.inc(ins, f"D{2 + s}", 16))

        exp_dma(0)
        exp_done = 0
        for g in range(PAIRS):
            s = g % 2
            if g + 1 < PAIRS:
                exp_dma(g + 1)
            for j in range(2):
                m = 2 * g + j
                bk = m % 2
                sem, v = bank_free[bk]
                p.wait(te, sem, v)
                p.wait(te, *exp_ready[g])
                for k in range(LATC):
                    it = te.matmul(
                        pbf[bk],
                        exp_sl[s][:, (j * LATC + k) * 128 : (j * LATC + k + 1) * 128],
                        lat[:, k * 512 : (k + 1) * 512],
                        start=(k == 0), stop=(k == LATC - 1))
                c_mm = p.inc(it, "SP")
                exp_done = c_mm
                exp_evac(m, bk, c_mm)
            exp_slot_last[s] = exp_done

        up_first = [half == 1]

        def up_evac(m, bk, c_mm):
            par = m % 2
            p.wait(se, "SP", c_mm)
            if up_first[0]:
                p.wait(se, "SV")
                up_first[0] = False
            se.activation(rsc[par], pbf[bk], AF.Relu)
            ia = se.activation(hup[:, m * 512 : (m + 1) * 512], rsc[par], AF.Square)
            bank_free[bk] = ("SC", p.inc(ia, "SC"))

        up_done = p1(INTC, DC, lambda m: dup[m].rearrange("p a b -> p (a b)"),
                     lambda k: t2Th[:, k * 512 : (k + 1) * 512],
                     up_evac, up_sl, [exp_done] * 3, ("D0", "D1", "D4"))
        hup_ready = p.cnt["SC"]
        p.wait(te, "SC", hup_ready)
        p.wait(te, "SV")

        f2_slot_last = [up_done, up_done]

        def down_dram(m):
            return ddown[m].rearrange("p a b -> p (a b)")

        def down_extra(m, bk):
            s = (m // 2) % 2
            j = m % 2
            for k in range(LATC):
                it = te.matmul(pbf[bk],
                               f2_sl[s][:, (j * LATC + k) * 128 : (j * LATC + k + 1) * 128],
                               moe[:, k * 512 : (k + 1) * 512],
                               start=False, stop=(k == LATC - 1))
            return it

        down_first = [half == 1]

        def down_evac(m, bk, c_mm):
            par = m % 2
            p.wait(ve, "SP", c_mm)
            if down_first[0]:
                p.wait(ve, "SC")
                down_first[0] = False
            iv = ve.tensor_copy(sch[par], pbf[bk])
            c_sch = p.inc(iv, "SV")
            bank_free[bk] = ("SV", c_sch)
            p.wait(te, "SV", c_sch)
            p.wait(te, "SV", tb_free[par])
            for j in range(4):
                it = te.transpose(pbb[4 + par][:, j * 128 : (j + 1) * 128],
                                  sch[par][:, j * 128 : (j + 1) * 128], id_bf)
            c_t = p.inc(it, "SP")
            p.wait(ve, "SP", c_t)
            for j in range(4):
                iv = ve.tensor_copy(stm[j][:, m * 128 : (m + 1) * 128],
                                    pbb[4 + par][:, j * 128 : (j + 1) * 128])
            tb_free[par] = p.inc(iv, "SV")

        slot_last_d = [up_done, up_done]
        slot_last_f = [up_done, up_done]
        down_ready = {}
        f2_ready = {}

        def down_dma(m):
            s = m % 2
            p.wait(po, "SP", slot_last_d[s])
            down_ready[m] = (f"D{s}", p.inc(po.dma_start(dw_sl[s], down_dram(m)), f"D{s}", 16))

        def f2_dma(g):
            s = g % 2
            p.wait(po, "SP", slot_last_f[s])
            ins = po.dma_start(
                f2_sl[s].rearrange("p (g c) -> p g c", g=2),
                dfc2[2 * g : 2 * g + 2].rearrange("g p a b -> p g (a b)"))
            f2_ready[g] = (f"D{2 + s}", p.inc(ins, f"D{2 + s}", 16))

        down_dma(0)
        f2_dma(0)
        down_done = 0
        for m in range(DC):
            s = m % 2
            if m + 1 < DC:
                down_dma(m + 1)
            if m % 2 == 0 and m // 2 + 1 < LATC:
                f2_dma(m // 2 + 1)
            bk = m % 2
            sem, v = bank_free[bk]
            p.wait(te, sem, v)
            p.wait(te, *down_ready[m])
            p.wait(te, *f2_ready[m // 2])
            for k in range(INTC):
                te.matmul(pbf[bk], dw_sl[s][:, k * 128 : (k + 1) * 128],
                          hup[:, k * 512 : (k + 1) * 512],
                          start=(k == 0), stop=False)
            it = down_extra(m, bk)
            c_mm = p.inc(it, "SP")
            slot_last_d[s] = c_mm
            if m % 2 == 1:
                slot_last_f[(m // 2) % 2] = c_mm
            down_done = c_mm
            down_evac(m, bk, c_mm)
        s_ready = p.cnt["SV"]
        if half == 0:
            p.wait(sy, "SP", down_done)
            load_t2(1)

        p.wait(ve, "SV", s_ready)
        p.wait(se, "SV", s_ready)
        p.wait(se, "SP", down_done)
        out_dma = [0, 0]
        first_pc = True
        for tl in range(4):
            t = half * 4 + tl
            c_r = ln_stats(stm[tl], trash)
            p.wait(ve, "SC", c_r)
            ve.tensor_scalar(ftile, stm[tl], stat(S_MU), stat(S_R),
                             ALU.subtract, ALU.mult)
            ve.tensor_tensor(ftile, ftile, gb_g, ALU.mult)
            iv = ve.tensor_tensor(ftile, ftile, gb_b, ALU.add)
            c_f = p.inc(iv, "SV")
            p.wait(se, "SV", c_f)
            c_r2 = ln_stats(ftile, trash)
            p.wait(ve, "SC", c_r2)
            for pc in range(2):
                if first_pc:
                    p.wait(sy, "SP", down_done)
                    first_pc = False
                c_x2 = p.inc(sy.dma_start(x2pc[pc], dx2sp[t][:, pc * 2048 : (pc + 1) * 2048]),
                             f"SX{pc}", 16)
                p.wait(ve, f"SS{pc}", out_dma[pc])
                ve.tensor_scalar(gpc[pc], ftile[:, pc * 2048 : (pc + 1) * 2048],
                                 stat(S_MU), stat(S_R), ALU.subtract, ALU.mult)
                p.wait(ve, f"SX{pc}", c_x2)
                iv = ve.tensor_tensor(gpc[pc], gpc[pc], x2pc[pc], ALU.add)
                c_o = p.inc(iv, "SV")
                p.wait(sy, "SV", c_o)
                ins = sy.dma_start(dout[t * 128 : (t + 1) * 128, pc * 2048 : (pc + 1) * 2048],
                                   gpc[pc])
                out_dma[pc] = p.inc(ins, f"SS{pc}", 16)
        if half == 0 and dtaps:
            p.wait(sy, "SA")
            for nm, buf, cols in (("lat", lat, LATC * 512), ("moe", moe, LATC * 512),
                                  ("hup", hup, INTC * 512)):
                if nm in dtaps:
                    p.wait(sy, "SV", p.cnt["SV"])
                    p.inc(sy.dma_start(dtaps[nm], buf), "SA", 16)
            if "stm" in dtaps:
                p.wait(sy, "SV", p.cnt["SV"])
                for j in range(4):
                    p.inc(sy.dma_start(dtaps["stm"][j * 128 : (j + 1) * 128, :], stm[j]),
                          "SA", 16)

    return finish()


def prep_weights(inp):
    w = {}
    cw = np.asarray(inp["L0_attn_conv_w"])
    w["wconv"] = np.ascontiguousarray(
        cw.reshape(3, DC, 128, W3C, 128).transpose(3, 0, 2, 1, 4))
    for nm, key in (("wq", "L0_attn_wq"), ("wk", "L0_attn_wk"), ("wv", "L0_attn_wv")):
        w[nm] = np.ascontiguousarray(np.asarray(inp[key]).reshape(WDC, 128, 1024))
    w["wo"] = np.ascontiguousarray(np.asarray(inp["L0_attn_wo"]).reshape(PAIRS, 128, 1024))
    w["projw"] = np.ascontiguousarray(
        np.asarray(inp["L0_attn_proj_w"]).reshape(WDC, 128, DC, 128).transpose(2, 1, 0, 3))
    w["gater"] = np.ascontiguousarray(
        np.asarray(inp["L0_ffn_gate_w"]).reshape(DC, 128, NE).transpose(1, 0, 2))
    w["upw"] = np.ascontiguousarray(
        np.asarray(inp["L0_ffn_up_w"]).reshape(DC, 128, INTC, 128).transpose(2, 1, 0, 3))
    w["downw"] = np.ascontiguousarray(
        np.asarray(inp["L0_ffn_down_w"]).reshape(INTC, 128, DC, 128).transpose(2, 1, 0, 3))
    w["fc1w"] = np.ascontiguousarray(
        np.asarray(inp["L0_ffn_fc1_w"]).reshape(DC, 128, LATC, 128).transpose(2, 1, 0, 3))
    w["expw"] = np.ascontiguousarray(
        np.asarray(inp["L0_ffn_expert_w"]).reshape(LATC, 128, LATC, 128).transpose(2, 1, 0, 3))
    w["fc2w"] = np.ascontiguousarray(
        np.asarray(inp["L0_ffn_fc2_w"]).reshape(LATC, 128, DC, 128).transpose(2, 1, 0, 3))
    w["convb"] = np.ascontiguousarray(np.asarray(inp["L0_attn_conv_b"]).reshape(W3C, 128))
    w["bq"] = np.ascontiguousarray(np.asarray(inp["L0_attn_bq"]).reshape(16, 64))
    w["bk"] = np.ascontiguousarray(np.asarray(inp["L0_attn_bk"]).reshape(16, 64))
    w["bv"] = np.ascontiguousarray(np.asarray(inp["L0_attn_bv"]).reshape(1, 1024))
    w["bo"] = np.ascontiguousarray(np.asarray(inp["L0_attn_bo"]).reshape(PAIRS, 128))
    w["projb"] = np.ascontiguousarray(np.asarray(inp["L0_attn_proj_b"]).reshape(DC, 128))
    w["gamma"] = np.ascontiguousarray(np.asarray(inp["L0_ffn_ln_gamma"]).reshape(1, D))
    w["beta"] = np.ascontiguousarray(np.asarray(inp["L0_ffn_ln_beta"]).reshape(1, D))
    import ml_dtypes
    w["id_bf"] = np.eye(128, dtype=ml_dtypes.bfloat16)
    w["id_f32"] = np.eye(128, dtype=np.float32)
    tril = np.tril(np.ones((128, 128), bool))
    w["mask"] = np.where(tril, np.float32(0.0), np.float32(NEG))
    return w


def core_in_map(inp, w, core):
    x_full = np.asarray(inp["L0_attn_inputs"])
    n, part = core // 4, core % 4
    s0 = part * 1024
    x = np.ascontiguousarray(x_full[n, s0 : s0 + 1024])
    halo = np.zeros((2, D), np.float32)
    if s0 > 0:
        halo[0] = x_full[n, s0 - 1]
    if s0 + T < 4096:
        halo[1] = x_full[n, s0 + T]
    m = {"x": x, "xhalo": halo}
    m.update(w)
    return m


def kernel(**inputs):
    w = prep_weights(inputs)
    nc = build()
    in_maps = [core_in_map(inputs, w, c) for c in range(8)]
    res = run_bass_kernel_spmd(nc, in_maps, core_ids=list(range(8)))
    out = np.concatenate([res.results[c]["out"] for c in range(8)], axis=0)
    return out.reshape(2, 4096, D)


# version 10
# speedup vs baseline: 1.1816x; 1.1816x over previous
import numpy as np

import concourse.bass as bass
import concourse.mybir as mybir
from concourse.bass_utils import run_bass_kernel_spmd

F32 = mybir.dt.float32
BF16 = mybir.dt.bfloat16
U8 = mybir.dt.uint8
AF = mybir.ActivationFunctionType
ALU = mybir.AluOpType
AX = mybir.AxisListType

T = 1024
NT = 8
D = 4096
DC = 32
W3 = 3072
W3C = 24
WDC = 8
HD = 16
PAIRS = 8
INT = 8192
INTC = 64
LAT = 2048
LATC = 16
NE = 8
EPS = 1e-5
NEG = -1e9

POOL_B = 207 * 1024
C2_MODE = 3

CO_ID_BF = 0
CO_ID_F32 = 256
CO_MASK = 768
CO_ONES_BF = 1280
CO_ONES_F32 = 1536
CO_CONVBT = 2048
CO_BQT = 2176
CO_BKT = 2208
CO_BOT = 2240
CO_PROJBT = 2272
CO_BV_ROW = 2400
CO_BVH = 6496
CO_BVL = 8544
CO_STATS = 10592
CO_EPS = 10848
CO_SMALLROWS = 10880
CO_BQ64 = 11392
CO_BK64 = 11456
CO_END = 11520

R_T1T = 12288
R_CFM = 77952
RA_XIN0 = 77952
RA_XIN1 = 94336
RA_T1S0 = 110720
RA_T1S1 = 118912
R_CONVSL = 127104
N_CONVSL = 5
R_WQ = 168064
R_WK = 184448

RC_WV = 12288
RC_WO = 28672
RC_VH = 45056
RC_P32B = 61440
RC_PEXPB = 69632
RC_PTSB = 73728

RC_QH64 = 94336
RC_KH64 = 127104
RC_ATTNW = 12288
RC_OALL = 77952

RE_AFM = 110720
RE_PROJSL = 176256
RF_TRASH = 176256
RA_TRASH = 168064
N_PROJSL = 4

RF_XIN0 = 12288
RF_XIN1 = 28672
RF_X2T0 = 45056
RF_X2T1 = 61440
RF_ATM0 = 77952
RF_ATM1 = 86144
RF_T2S0 = 94336
RF_T2S1 = 102528

RG_GB_G = 12288
RG_GB_B = 20480
RG_T2TH = 28672
RG_LAT = 61440
RG_MOE = 77824
RG_HUP = 94208
RG_STM = 159744
RG_SCH0 = 192512
RG_SCH1 = 193536
RG_LOGI0 = 194560
RG_LOGI1 = 194592
RG_MSK8 = 194624
RG_RWROW = 194688
RG_RWB = 196736
RG_GATE = 198784
RG_RSC0 = 199296
RG_RSC1 = 201344
RG_UPSL = 61440
RG_FC1SL = 94208
RG_EXPSL = 110592
RG_DWSL = 28672
RG_F2SL = 61440
RG_FT = 126976
RG_TRASH = 143360
RG_GROW = 126976
RG_BROW = 143360
RG_X2PC0 = 77824
RG_X2PC1 = 86016
RG_GPC0 = 61440
RG_GPC1 = 69632
RG_UPSL3 = 203392

assert RG_UPSL3 + 8192 <= POOL_B
assert RG_RSC1 + 2048 <= POOL_B
assert R_WK + 16384 <= POOL_B


class P:

    def __init__(self, nc):
        self.nc = nc
        self.sems = {}
        self.cnt = {}
        self._auto = set()
        names = ["SA", "SP", "SV", "SC", "SG", "SX0", "SX1", "SS0", "SS1",
                 "WQ", "WK", "WV", "WO"] + [f"D{i}" for i in range(5)]
        for name in names:
            self.sems[name] = nc.alloc_semaphore(name=name)
            self.cnt[name] = 0

    def inc(self, instr, sem, n=1):
        if isinstance(instr, _Auto):
            return self.cnt[sem]
        instr.then_inc(self.sems[sem], n)
        self.cnt[sem] += n
        return self.cnt[sem]

    def wait(self, eng, sem, val=None):
        v = self.cnt[sem] if val is None else val
        if v > 0:
            eng.wait_ge(self.sems[sem], v)


class _Auto:

    def __init__(self, instr):
        self.instr = instr


class EngWrap:

    def __init__(self, p, eng, sem):
        object.__setattr__(self, "_p", p)
        object.__setattr__(self, "_e", eng)
        object.__setattr__(self, "_s", sem)

    def wait_ge(self, sem, v):
        return self._e.wait_ge(sem, v)

    def __getattr__(self, name):
        fn = getattr(self._e, name)
        p, eng, semname = self._p, self._e, self._s

        def wrapped(*a, **k):
            p.wait(eng, semname)
            instr = fn(*a, **k)
            instr.then_inc(p.sems[semname], 1)
            p.cnt[semname] += 1
            return _Auto(instr)

        return wrapped


def build(taps=(), stop_phase=99):
    nc = bass.Bass("TRN2", target_bir_lowering=False, debug=False)
    p = P(nc)
    te, sy, po = nc.tensor, nc.sync, nc.gpsimd
    ve = EngWrap(p, nc.vector, "SV")
    se = EngWrap(p, nc.scalar, "SC")

    dx = nc.dram_tensor("x", [T, D], F32, kind="ExternalInput").ap()
    dxh = nc.dram_tensor("xhalo", [2, D], F32, kind="ExternalInput").ap()
    dwconv = nc.dram_tensor("wconv", [W3C, 3, 128, DC, 128], F32, kind="ExternalInput").ap()
    dwq = nc.dram_tensor("wq", [WDC, 128, 1024], F32, kind="ExternalInput").ap()
    dwk = nc.dram_tensor("wk", [WDC, 128, 1024], F32, kind="ExternalInput").ap()
    dwv = nc.dram_tensor("wv", [WDC, 128, 1024], F32, kind="ExternalInput").ap()
    dwo = nc.dram_tensor("wo", [PAIRS, 128, 1024], F32, kind="ExternalInput").ap()
    dproj = nc.dram_tensor("projw", [DC, 128, WDC, 128], F32, kind="ExternalInput").ap()
    dgate = nc.dram_tensor("gater", [128, DC, NE], F32, kind="ExternalInput").ap()
    dup = nc.dram_tensor("upw", [INTC, 128, DC, 128], F32, kind="ExternalInput").ap()
    ddown = nc.dram_tensor("downw", [DC, 128, INTC, 128], F32, kind="ExternalInput").ap()
    dfc1 = nc.dram_tensor("fc1w", [LATC, 128, DC, 128], F32, kind="ExternalInput").ap()
    dexp = nc.dram_tensor("expw", [LATC, 128, LATC, 128], F32, kind="ExternalInput").ap()
    dfc2 = nc.dram_tensor("fc2w", [DC, 128, LATC, 128], F32, kind="ExternalInput").ap()
    dconvb = nc.dram_tensor("convb", [W3C, 128], F32, kind="ExternalInput").ap()
    dbq = nc.dram_tensor("bq", [16, 64], F32, kind="ExternalInput").ap()
    dbk = nc.dram_tensor("bk", [16, 64], F32, kind="ExternalInput").ap()
    dbv = nc.dram_tensor("bv", [1, 1024], F32, kind="ExternalInput").ap()
    dbo = nc.dram_tensor("bo", [PAIRS, 128], F32, kind="ExternalInput").ap()
    dprojb = nc.dram_tensor("projb", [DC, 128], F32, kind="ExternalInput").ap()
    dgam = nc.dram_tensor("gamma", [1, D], F32, kind="ExternalInput").ap()
    dbet = nc.dram_tensor("beta", [1, D], F32, kind="ExternalInput").ap()
    didb = nc.dram_tensor("id_bf", [128, 128], BF16, kind="ExternalInput").ap()
    didf = nc.dram_tensor("id_f32", [128, 128], F32, kind="ExternalInput").ap()
    dmask = nc.dram_tensor("mask", [128, 128], F32, kind="ExternalInput").ap()
    dout = nc.dram_tensor("out", [T, D], F32, kind="ExternalOutput").ap()
    dx2sp = nc.dram_tensor("x2sp", [NT, 128, D], F32, kind="Internal").ap()
    dt2sp = nc.dram_tensor("t2sp", [NT, 128, DC, 128], BF16, kind="Internal").ap()
    dtaps = {}
    for tname, shape, dt in taps:
        dtaps[tname] = nc.dram_tensor("tap_" + tname, shape, dt, kind="ExternalOutput").ap()

    pool = nc.alloc_sbuf_tensor("pool", [128, POOL_B], U8).ap()

    def bF(off, cols, parts=128):
        apv = pool[:, off : off + cols * 4].bitcast(F32)
        return apv if parts == 128 else apv[0:parts]

    def bB(off, cols, parts=128):
        apv = pool[:, off : off + cols * 2].bitcast(BF16)
        return apv if parts == 128 else apv[0:parts]

    ps8 = nc.alloc_psum_tensor("ps8", [128, 4096], F32).ap()
    ps8b = ps8.bitcast(BF16)
    pbf = [ps8[:, i * 512 : (i + 1) * 512] for i in range(8)]
    pbb = [ps8b[:, i * 1024 : (i + 1) * 1024] for i in range(8)]

    id_bf = bB(CO_ID_BF, 128)
    id_f32 = bF(CO_ID_F32, 128)
    mask = bF(CO_MASK, 128)
    ones_bf = bB(CO_ONES_BF, 128, parts=1)
    ones_f32 = bF(CO_ONES_F32, 128, parts=1)
    convbT = bF(CO_CONVBT, 24)
    bq64 = bF(CO_BQ64, 16, parts=64)
    bk64 = bF(CO_BK64, 16, parts=64)
    boT = bF(CO_BOT, 8)
    projbT = bF(CO_PROJBT, 32)
    bv_row = bF(CO_BV_ROW, 1024, parts=1)
    bvh = bB(CO_BVH, 1024, parts=1)
    bvl = bB(CO_BVL, 1024, parts=1)
    stats = bF(CO_STATS, 64)
    eps_col = bF(CO_EPS, 1)
    srows = bF(CO_SMALLROWS, 128, parts=32)

    def stat(i):
        return stats[:, i : i + 1]

    nc.const_aps.register(F32, EPS, eps_col)

    p.inc(sy.dma_start(id_bf, didb), "SA", 16)
    p.inc(sy.dma_start(id_f32, didf), "SA", 16)
    p.inc(sy.dma_start(mask, dmask), "SA", 16)
    p.inc(sy.dma_start(bv_row, dbv), "SA", 16)
    p.inc(sy.dma_start(srows[0:24], dconvb), "SA", 16)
    ve.memset(ones_f32, 1.0)
    ve.memset(bB(CO_ONES_BF, 128, parts=1), 1.0)
    ve.memset(eps_col, EPS)
    p.wait(ve, "SA")
    ve.tensor_copy(bvh, bv_row)
    p.inc(ve.tensor_tensor(bvl, bv_row, bvh, ALU.subtract), "SV")

    def small_T(n, dst_f32):
        p.wait(te, "SA")
        p.wait(te, "SV")
        it = te.transpose(pbf[0][:, 0:n], srows[0:n], id_f32[0:n, 0:n])
        c = p.inc(it, "SP")
        p.wait(ve, "SP", c)
        p.inc(ve.tensor_copy(dst_f32, pbf[0][:, 0:n]), "SV")
        return c

    c_st = small_T(24, convbT)
    for dsrc, n, dst in ((dbo, 8, boT), (dprojb, 32, projbT)):
        p.wait(sy, "SP", c_st)
        p.inc(sy.dma_start(srows[0:n], dsrc), "SA", 16)
        c_st = small_T(n, dst)
    for dsrc, dst in ((dbq, bq64), (dbk, bk64)):
        p.wait(sy, "SP", c_st)
        p.inc(sy.dma_start(srows[0:16, 0:64], dsrc), "SA", 16)
        p.wait(te, "SA")
        p.wait(te, "SV")
        it = te.transpose(pbf[0][0:64, 0:16], srows[0:16, 0:64], id_f32[0:16, 0:16])
        c_st = p.inc(it, "SP")
        p.wait(ve, "SP", c_st)
        p.inc(ve.tensor_copy(dst, pbf[0][0:64, 0:16]), "SV")

    S_SUM, S_SQ, S_MU, S_VAR, S_R, S_NEGM, S_L, S_M1, S_M2, S_RW = range(10)

    def finish():
        for s in ("SA", "SX0", "SX1", "SS0", "SS1", "WQ", "WK", "WV", "WO",
                  "D0", "D1", "D2", "D3", "D4"):
            p.wait(sy, s)
        return nc

    def tap_barrier():
        if dtaps:
            for q in (te, sy, po, nc.vector, nc.scalar):
                p.wait(q, "SA")

    def ln_stats_fast(src, sq_trash, sum_trash, width=D):
        iv = ve.tensor_scalar(sum_trash, src, 1.0, 0.0, ALU.mult, ALU.add,
                              accum_out=stat(S_SUM))
        p.inc(iv, "SV")
        ia = se.activation(sq_trash, src, AF.Square, accum_out=stat(S_SQ))
        c_sq = p.inc(ia, "SC")
        ve.tensor_scalar(stat(S_MU), stat(S_SUM), 1.0 / width, None, ALU.mult)
        p.wait(ve, "SC", c_sq)
        ve.tensor_tensor(stat(S_VAR), stat(S_MU), stat(S_MU), ALU.mult)
        iv = ve.tensor_scalar(stat(S_VAR), stat(S_SQ), 1.0 / width, stat(S_VAR),
                              ALU.mult, ALU.subtract)
        c_v = p.inc(iv, "SV")
        p.wait(se, "SV", c_v)
        se.activation(stat(S_R), stat(S_VAR), AF.Ln, bias=EPS, scale=1.0)
        return p.inc(se.activation(stat(S_R), stat(S_R), AF.Exp, bias=0.0, scale=-0.5), "SC")

    def ln_stats(src, sq_trash, width=D):
        p.inc(ve.reduce_sum(stat(S_SUM), src, axis=AX.X), "SV")
        ia = se.activation(sq_trash, src, AF.Square, accum_out=stat(S_SQ))
        c_sq = p.inc(ia, "SC")
        ve.tensor_scalar(stat(S_MU), stat(S_SUM), 1.0 / width, None, ALU.mult)
        p.wait(ve, "SC", c_sq)
        ve.tensor_tensor(stat(S_VAR), stat(S_MU), stat(S_MU), ALU.mult)
        iv = ve.tensor_scalar(stat(S_VAR), stat(S_SQ), 1.0 / width, stat(S_VAR),
                              ALU.mult, ALU.subtract)
        c_v = p.inc(iv, "SV")
        p.wait(se, "SV", c_v)
        se.activation(stat(S_R), stat(S_VAR), AF.Ln, bias=EPS, scale=1.0)
        return p.inc(se.activation(stat(S_R), stat(S_R), AF.Exp, bias=0.0, scale=-0.5), "SC")

    t1T = bB(R_T1T, DC * 1026)
    t1T3 = t1T.rearrange("p (k c) -> p k c", k=DC)
    trashA = bF(RA_TRASH, D)
    xin = [bF(RA_XIN0, D), bF(RA_XIN1, D)]
    t1s = [bB(RA_T1S0, D), bB(RA_T1S1, D)]
    xin_free = [0, 0]
    t1s_free = [0, 0]
    tb_free = [0, 0]

    def tm_to_fm(src_bf, dst_groups):
        for g in range(8):
            tb = 4 + (g % 2)
            p.wait(te, "SV", tb_free[g % 2])
            for j in range(4):
                ch = g * 4 + j
                it = te.transpose(pbb[tb][:, j * 128 : (j + 1) * 128],
                                  src_bf[:, ch * 128 : (ch + 1) * 128], id_bf)
            c = p.inc(it, "SP")
            p.wait(ve, "SP", c)
            iv = ve.tensor_copy(dst_groups[g],
                                pbb[tb].rearrange("p (a b) -> p a b", a=8)[:, 0:4])
            tb_free[g % 2] = p.inc(iv, "SV")

    for t in range(NT + 1):
        par = t % 2
        if t == NT:
            p.wait(ve, "SV", xin_free[par])
            c = p.inc(ve.memset(xin[par], 0.0), "SV")
            p.wait(sy, "SV", c)
            ins = sy.dma_start(xin[par][0:2], dxh)
        else:
            p.wait(sy, "SV", xin_free[par])
            ins = sy.dma_start(xin[par], dx[t * 128 : (t + 1) * 128])
        c_load = p.inc(ins, f"SX{par}", 16)
        p.wait(ve, f"SX{par}", c_load)
        p.wait(se, f"SX{par}", c_load)
        p.wait(se, "SP", t1s_free[par])
        c_r = ln_stats_fast(xin[par], t1s[par], trashA)
        p.wait(ve, "SC", c_r)
        iv = ve.tensor_scalar(t1s[par], xin[par], stat(S_MU), stat(S_R),
                              ALU.subtract, ALU.mult)
        c_n = p.inc(iv, "SV")
        xin_free[par] = c_n
        p.wait(te, "SV", c_n)
        if t < NT:
            tm_to_fm(t1s[par],
                     [t1T3[:, g * 4 : g * 4 + 4, 1 + t * 128 : 1 + (t + 1) * 128]
                      for g in range(8)])
        else:
            for g in range(8):
                tb = 4 + (g % 2)
                p.wait(te, "SV", tb_free[g % 2])
                for j in range(4):
                    ch = g * 4 + j
                    it = te.transpose(pbb[tb][:, j * 128 : (j + 1) * 128],
                                      t1s[par][:, ch * 128 : (ch + 1) * 128], id_bf)
                c = p.inc(it, "SP")
                p.wait(ve, "SP", c)
                src3 = pbb[tb].rearrange("p (a b) -> p a b", a=8)
                ve.tensor_copy(t1T3[:, g * 4 : g * 4 + 4, 0:1], src3[:, 0:4, 0:1])
                iv = ve.tensor_copy(t1T3[:, g * 4 : g * 4 + 4, 1025:1026], src3[:, 0:4, 1:2])
                tb_free[g % 2] = p.inc(iv, "SV")
        t1s_free[par] = p.cnt["SP"]
    t1T_ready = p.cnt["SV"]

    if "t1T" in dtaps:
        p.wait(sy, "SA")
        p.wait(sy, "SV", t1T_ready)
        p.inc(sy.dma_start(dtaps["t1T"], t1T), "SA", 16)
    if stop_phase <= 0:
        return finish()

    c_fm = bB(R_CFM, W3C * 1024)
    conv_sl = [bB(R_CONVSL + s * 8192, DC * 128) for s in range(N_CONVSL)]
    wq_sb = bB(R_WQ, WDC * 1024)
    wk_sb = bB(R_WK, WDC * 1024)
    slot_last = [0] * N_CONVSL
    group_ready = {}
    bank_free = {i: ("SC", 0) for i in range(4)}

    p.wait(te, "SV", t1T_ready)
    gi = 0
    for m in range(W3C):
        for dw in range(3):
            s = gi % N_CONVSL
            p.wait(po, "SP", slot_last[s])
            ins = po.dma_start(conv_sl[s], dwconv[m, dw].rearrange("p a b -> p (a b)"))
            group_ready[gi] = (f"D{s}", p.inc(ins, f"D{s}", 16))
            gi += 1
        for half in range(2):
            bk = (m * 2 + half) % 4
            sem, v = bank_free[bk]
            p.wait(te, sem, v)
            for dw in range(3):
                g = m * 3 + dw
                p.wait(te, *group_ready[g])
                sl = conv_sl[g % N_CONVSL]
                base = half * 512 + dw
                for k in range(DC):
                    it = te.matmul(pbf[bk], sl[:, k * 128 : (k + 1) * 128],
                                   t1T[:, k * 1026 + base : k * 1026 + base + 512],
                                   start=(dw == 0 and k == 0),
                                   stop=(dw == 2 and k == DC - 1))
            c_mm = p.inc(it, "SP")
            if half == 1:
                for dw in range(3):
                    slot_last[(m * 3 + dw) % N_CONVSL] = c_mm
            p.wait(se, "SP", c_mm)
            ia = se.activation(c_fm[:, m * 1024 + half * 512 : m * 1024 + half * 512 + 512],
                               pbf[bk], AF.Identity, bias=convbT[:, m : m + 1], scale=1.0)
            bank_free[bk] = ("SC", p.inc(ia, "SC"))
    c_ready = p.cnt["SC"]
    p.inc(po.dma_start(wq_sb.rearrange("p (k c) -> p k c", k=WDC), dwq.rearrange("k p c -> p k c")), "WQ", 16)
    p.inc(po.dma_start(wk_sb.rearrange("p (k c) -> p k c", k=WDC), dwk.rearrange("k p c -> p k c")), "WK", 16)


    if "c" in dtaps:
        p.wait(sy, "SA")
        p.wait(sy, "SC", c_ready)
        p.inc(sy.dma_start(dtaps["c"], c_fm), "SA", 16)
    if stop_phase <= 1:
        return finish()

    tap_barrier()
    wv_sb = bB(RC_WV, WDC * 1024)
    wo_sb = bB(RC_WO, PAIRS * 1024)
    vh_sb = bB(RC_VH, NT * 1024)
    o_all = bB(RC_OALL, PAIRS * 1024)
    qh64 = bB(RC_QH64, HD * 1024, parts=64)
    kh64 = bB(RC_KH64, HD * 1024, parts=64)
    attnW = bB(RC_ATTNW, PAIRS * 1024)
    p32b = bF(RC_P32B, 2048)
    pexpb = bB(RC_PEXPB, 2048)
    ptsb = bB(RC_PTSB, 2048)
    negm16 = stats[:, 16:32]
    linv16 = stats[:, 32:48]
    p32h = p32b.rearrange("p (h c) -> p h c", h=HD)
    pexph = pexpb.rearrange("p (h c) -> p h c", h=HD)
    mask_b = mask.unsqueeze(1).broadcast_to([128, HD, 128])
    ps_sc = ps8[:, 1024:3072]

    p.wait(po, "SP")
    p.inc(po.dma_start(wv_sb.rearrange("p (k c) -> p k c", k=WDC), dwv.rearrange("k p c -> p k c")), "WV", 16)
    p.inc(po.dma_start(wo_sb.rearrange("p (k c) -> p k c", k=PAIRS), dwo.rearrange("k p c -> p k c")), "WO", 16)

    fb = {0: bank_free[0], 1: bank_free[1]}
    fb_next = [0]

    def fill_bank():
        bk = fb_next[0]
        fb_next[0] ^= 1
        sem, v = fb[bk]
        p.wait(te, sem, v)
        return bk

    first_flags = {"qk": True, "vh": True, "wo": True}

    def u_qk(which, pr, half):
        if first_flags["qk"]:
            p.wait(te, "WQ")
            p.wait(te, "WK")
            p.wait(te, "SC", c_ready)
            first_flags["qk"] = False
        w_sb = wq_sb if which == 0 else wk_sb
        b64 = bq64 if which == 0 else bk64
        dst = qh64 if which == 0 else kh64
        bk = fill_bank()
        for k in range(WDC):
            it = te.matmul(pbf[bk], w_sb[:, k * 1024 + pr * 128 : k * 1024 + (pr + 1) * 128],
                           c_fm[:, (which * 8 + k) * 1024 + half * 512
                                : (which * 8 + k) * 1024 + half * 512 + 512],
                           start=(k == 0), stop=(k == WDC - 1))
        c_mm = p.inc(it, "SP")
        p.wait(se, "SP", c_mm)
        for j in range(2):
            h = 2 * pr + j
            ia = se.activation(dst[:, h * 1024 + half * 512 : h * 1024 + half * 512 + 512],
                               pbf[bk][j * 64 : (j + 1) * 64, 0:512], AF.Identity,
                               bias=b64[:, h : h + 1], scale=1.0)
        fb[bk] = ("SC", p.inc(ia, "SC"))

    vh_done = {}

    def u_vh(b, hv):
        if first_flags["vh"]:
            p.wait(te, "WV")
            first_flags["vh"] = False
        bk = fill_bank()
        for k in range(WDC):
            te.matmul(pbf[bk],
                      c_fm[:, (16 + k) * 1024 + b * 128 : (16 + k) * 1024 + (b + 1) * 128],
                      wv_sb[:, k * 1024 + hv * 512 : k * 1024 + hv * 512 + 512],
                      start=(k == 0), stop=False)
        te.matmul(pbf[bk], ones_bf, bvh[:, hv * 512 : hv * 512 + 512],
                  start=False, stop=False)
        it = te.matmul(pbf[bk], ones_bf, bvl[:, hv * 512 : hv * 512 + 512],
                       start=False, stop=True)
        c_mm = p.inc(it, "SP")
        p.wait(ve, "SP", c_mm)
        iv = ve.tensor_copy(vh_sb[:, b * 1024 + hv * 512 : b * 1024 + hv * 512 + 512],
                            pbf[bk])
        fb[bk] = ("SV", p.inc(iv, "SV"))
        if hv == 1:
            vh_done[b] = p.cnt["SV"]

    def u_wo(m, half, oev_ready):
        if first_flags["wo"]:
            p.wait(te, "WO")
            first_flags["wo"] = False
        bk = fill_bank()
        p.wait(te, "SV", oev_ready)
        for pr in range(PAIRS):
            it = te.matmul(pbf[bk],
                           wo_sb[:, pr * 1024 + m * 128 : pr * 1024 + (m + 1) * 128],
                           o_all[:, pr * 1024 + half * 512 : pr * 1024 + half * 512 + 512],
                           start=(pr == 0), stop=(pr == PAIRS - 1))
        c_mm = p.inc(it, "SP")
        p.wait(se, "SP", c_mm)
        ia = se.activation(attnW[:, m * 1024 + half * 512 : m * 1024 + half * 512 + 512],
                           pbf[bk], AF.Identity, bias=boT[:, m : m + 1], scale=1.0)
        fb[bk] = ("SC", p.inc(ia, "SC"))

    sc_mm, tr_mm = {}, {}
    stt_done, norm_done, pt_done, oev_last = {}, {}, {}, {}
    obank = {}
    qk_ready = {}

    def attn_scores(b):
        p.wait(te, "SV", stt_done.get(b - 1, 0))
        p.wait(te, "SC", qk_ready[b // 4])
        for h in range(HD):
            it = te.matmul(ps_sc[:, h * 128 : (h + 1) * 128],
                           qh64[:, h * 1024 + b * 128 : h * 1024 + (b + 1) * 128],
                           kh64[:, h * 1024 + b * 128 : h * 1024 + (b + 1) * 128],
                           start=True, stop=True)
        sc_mm[b] = p.inc(it, "SP")

    def attn_softmax(b):
        p.wait(ve, "SP", sc_mm[b])
        iv = ve.scalar_tensor_tensor(p32b, ps_sc, 0.125, mask_b, ALU.mult, ALU.add)
        stt_done[b] = p.inc(iv, "SV")
        ve.reduce_max(negm16, p32h, axis=AX.X, negate=True)
        ve.tensor_tensor(p32h, p32h,
                         negm16.unsqueeze(2).broadcast_to([128, HD, 128]), ALU.add)
        c_sub = p.cnt["SV"]
        p.wait(se, "SV", c_sub)
        p.wait(se, "SP", tr_mm.get(b - 1, 0))
        ia = se.activation(pexpb, p32b, AF.Exp)
        c_e = p.inc(ia, "SC")
        p.wait(ve, "SC", c_e)
        ve.reduce_sum(linv16, pexph, axis=AX.X)
        ve.reciprocal(linv16, linv16)
        iv = ve.tensor_tensor(pexph, pexph,
                              linv16.unsqueeze(2).broadcast_to([128, HD, 128]), ALU.mult)
        norm_done[b] = p.inc(iv, "SV")

    def attn_pt_o(b):
        p.wait(te, "SV", norm_done[b])
        p.wait(te, "SV", oev_last.get(b - 1, 0))
        for h in range(HD):
            it = te.transpose(ps8b[:, 6144 + h * 128 : 6144 + (h + 1) * 128],
                              pexpb[:, h * 128 : (h + 1) * 128], id_bf)
        tr_mm[b] = p.inc(it, "SP")
        p.wait(ve, "SP", tr_mm[b])
        iv = ve.tensor_copy(ptsb, ps8b[:, 6144:8192])
        pt_done[b] = p.inc(iv, "SV")
        p.wait(te, "SV", pt_done[b])
        p.wait(te, "SV", vh_done[b])
        for pr in range(PAIRS):
            bk6 = 6 + (pr % 2)
            p.wait(te, "SV", obank.get(bk6, 0))
            for j in range(2):
                h = 2 * pr + j
                it = te.matmul(pbf[bk6][j * 64 : (j + 1) * 64, 0:128],
                               vh_sb[:, b * 1024 + h * 64 : b * 1024 + (h + 1) * 64],
                               ptsb[:, h * 128 : (h + 1) * 128], start=True, stop=True)
            c_mm = p.inc(it, "SP")
            p.wait(ve, "SP", c_mm)
            iv = ve.tensor_copy(o_all[:, pr * 1024 + b * 128 : pr * 1024 + (b + 1) * 128],
                                pbf[bk6][:, 0:128])
            obank[bk6] = p.inc(iv, "SV")
            oev_last[b] = obank[bk6]

    for pr in range(PAIRS):
        u_qk(1, pr, 0)
        u_qk(1, pr, 1)
    for b in range(NT):
        u_vh(b, 0)
        u_vh(b, 1)
    for pr in range(PAIRS):
        u_qk(0, pr, 0)
        u_qk(0, pr, 1)
    qk_ready[0] = qk_ready[1] = p.cnt["SC"]

    units2 = None
    for b in range(NT):
        if b == 4:
            units2 = [lambda m=m: u_wo(m, 0, oev_last[3]) for m in range(WDC)]
        attn_scores(b)
        attn_softmax(b)
        if b >= 4 and units2:
            for _ in range(2):
                if units2:
                    units2.pop(0)()
        attn_pt_o(b)
    while units2:
        units2.pop(0)()
    o_ready = p.cnt["SV"]

    if "qh" in dtaps:
        p.wait(sy, "SA")
        p.wait(sy, "SC", qk_ready[1])
        p.inc(sy.dma_start(dtaps["qh"], qh64), "SA", 16)
        p.inc(sy.dma_start(dtaps["kh"], kh64), "SA", 16)
        p.wait(sy, "SV", o_ready)
        p.inc(sy.dma_start(dtaps["vh"], vh_sb), "SA", 16)
    if "o" in dtaps:
        p.wait(sy, "SA")
        p.wait(sy, "SV", o_ready)
        p.inc(sy.dma_start(dtaps["o"], o_all), "SA", 16)
    if stop_phase <= 3:
        return finish()

    tap_barrier()
    for m in range(WDC):
        u_wo(m, 1, oev_last[NT - 1])
    wo_done = p.cnt["SC"]

    a_fm = bB(RE_AFM, DC * 1024)
    proj_sl = [bB(RE_PROJSL + s * 2048, WDC * 128) for s in range(N_PROJSL)]
    slot_last = [p.cnt["SP"]] * N_PROJSL
    slot_sem = ["SP"] * N_PROJSL
    group_ready = {}

    def proj_dma(m):
        s = m % N_PROJSL
        p.wait(po, slot_sem[s], slot_last[s])
        ins = po.dma_start(proj_sl[s], dproj[m].rearrange("p a b -> p (a b)"))
        group_ready[m] = (f"D{s}", p.inc(ins, f"D{s}", 16))

    for m in range(N_PROJSL):
        proj_dma(m)
    p.wait(te, "SC", wo_done)
    for m in range(DC):
        s = m % N_PROJSL
        p.wait(te, *group_ready[m])
        for half in range(2):
            bk = fill_bank()
            for k in range(WDC):
                it = te.matmul(pbf[bk], proj_sl[s][:, k * 128 : (k + 1) * 128],
                               attnW[:, k * 1024 + half * 512 : k * 1024 + half * 512 + 512],
                               start=(k == 0), stop=(k == WDC - 1))
            c_mm = p.inc(it, "SP")
            p.wait(se, "SP", c_mm)
            ia = se.activation(a_fm[:, m * 1024 + half * 512 : m * 1024 + half * 512 + 512],
                               pbf[bk], AF.Identity, bias=projbT[:, m : m + 1], scale=1.0)
            fb[bk] = ("SC", p.inc(ia, "SC"))
        slot_last[s] = p.cnt["SP"]
        if m + N_PROJSL < DC:
            proj_dma(m + N_PROJSL)
    a_ready = p.cnt["SC"]
    bank_free = {i: ("SC", a_ready) for i in range(4)}

    if "a" in dtaps:
        p.wait(sy, "SA")
        p.wait(sy, "SC", a_ready)
        p.inc(sy.dma_start(dtaps["a"], a_fm), "SA", 16)
    if stop_phase <= 4:
        return finish()

    tap_barrier()
    xinF = [bF(RF_XIN0, D), bF(RF_XIN1, D)]
    x2t = [bF(RF_X2T0, D), bF(RF_X2T1, D)]
    stage = [bB(RF_X2T0, D), bB(RF_X2T1, D)]
    atm = [bB(RF_ATM0, D), bB(RF_ATM1, D)]
    t2s = [bB(RF_T2S0, D), bB(RF_T2S1, D)]
    trashF = bF(RF_TRASH, D)
    af = [0, 0]
    tb_free2 = [0, 0]
    xin_free = [0, 0]
    x2_spilled = [0, 0]
    t2s_free = [0, 0]
    atm_free = [0, 0]
    p.wait(sy, "SC", a_ready)
    p.wait(te, "SC", a_ready)
    c_t2s = {}
    c_spills = {}

    def f_part1(t):
        par = t % 2
        p.wait(sy, "SG", xin_free[par])
        c_load = p.inc(sy.dma_start(xinF[par], dx[t * 128 : (t + 1) * 128]), f"SX{par}", 16)
        atm3 = atm[par].rearrange("p (a b) -> p a b", a=DC)
        p.wait(te, "SG", atm_free[par])
        for g in range(8):
            tb = 4 + (g % 2)
            p.wait(te, "SC", af[g % 2])
            for j in range(4):
                m = g * 4 + j
                it = te.transpose(pbb[tb][:, j * 128 : (j + 1) * 128],
                                  a_fm[:, m * 1024 + t * 128 : m * 1024 + (t + 1) * 128],
                                  id_bf)
            c = p.inc(it, "SP")
            p.wait(se, "SP", c)
            ia = se.activation(atm3[:, g * 4 : g * 4 + 4],
                               pbb[tb].rearrange("p (a b) -> p a b", a=8)[:, 0:4],
                               AF.Identity)
            af[g % 2] = p.inc(ia, "SC")
        p.wait(ve, f"SX{par}", c_load)
        p.wait(se, f"SS{par}", x2_spilled[par])
        p.wait(ve, "SC", af[0])
        p.wait(ve, "SC", af[1])
        c_r = ln_stats_fast(atm[par], x2t[par], trashF)
        p.wait(ve, "SC", c_r)
        p.wait(ve, f"SS{par}", x2_spilled[par])
        iv = ve.tensor_scalar(x2t[par], atm[par], stat(S_MU), stat(S_R),
                              ALU.subtract, ALU.mult)
        c_g = p.inc(iv, "SV")
        p.wait(po, "SV", c_g)
        p.wait(po, f"SX{par}", c_load)
        it = po.tensor_tensor(x2t[par], x2t[par], xinF[par], ALU.add)
        c_x2 = p.inc(it, "SG")
        xin_free[par] = c_x2
        atm_free[par] = c_x2
        p.wait(sy, "SG", c_x2)
        c_spills[t] = p.inc(sy.dma_start(dx2sp[t], x2t[par]), f"SS{par}", 16)
        p.wait(se, "SG", c_x2)
        p.wait(ve, "SG", c_x2)
        p.wait(se, f"SS{par}", t2s_free[par])
        c_r2 = ln_stats_fast(x2t[par], t2s[par], trashF)
        p.wait(ve, "SC", c_r2)
        iv = ve.tensor_scalar(t2s[par], x2t[par], stat(S_MU), stat(S_R),
                              ALU.subtract, ALU.mult)
        c_t2s[t] = p.inc(iv, "SV")

    def f_part2(t):
        par = t % 2
        p.wait(te, "SV", c_t2s[t])
        p.wait(te, f"SS{par}", c_spills[t])
        st3 = stage[par].rearrange("p (a b) -> p a b", a=DC)
        for g in range(8):
            tb = 2 + (g % 2)
            p.wait(te, "SV", tb_free2[g % 2])
            for j in range(4):
                ch = g * 4 + j
                it = te.transpose(pbb[tb][:, j * 128 : (j + 1) * 128],
                                  t2s[par][:, ch * 128 : (ch + 1) * 128], id_bf)
            c = p.inc(it, "SP")
            p.wait(ve, "SP", c)
            iv = ve.tensor_copy(st3[:, g * 4 : g * 4 + 4],
                                pbb[tb].rearrange("p (a b) -> p a b", a=8)[:, 0:4])
            tb_free2[g % 2] = p.inc(iv, "SV")
        c_stage = p.cnt["SV"]
        p.wait(sy, "SV", c_stage)
        ins = sy.dma_start(dt2sp[t], stage[par].rearrange("p (k c) -> p k c", k=DC))
        t2s_free[par] = p.inc(ins, f"SS{par}", 16)
        x2_spilled[par] = t2s_free[par]

    for t in range(NT + 1):
        if t < NT:
            f_part1(t)
        if t >= 1:
            f_part2(t - 1)
    f_ss = (p.cnt["SS0"], p.cnt["SS1"])
    f_sv = p.cnt["SV"]

    if "x2" in dtaps:
        p.wait(sy, "SA")
        p.wait(sy, "SS0", f_ss[0]); p.wait(sy, "SS1", f_ss[1])
        p.inc(sy.dma_start(dtaps["x2"], dx2sp.rearrange("t p d -> (t p) d")), "SA", 16)
    if "t2sp" in dtaps:
        p.wait(sy, "SS0", f_ss[0]); p.wait(sy, "SS1", f_ss[1])
        p.inc(sy.dma_start(dtaps["t2sp"], dt2sp.rearrange("t p k c -> (t p) (k c)")), "SA", 16)
    if stop_phase <= 5:
        return finish()

    tap_barrier()
    gb_g = bB(RG_GB_G, D)
    gb_b = bB(RG_GB_B, D)
    grow = bF(RG_GROW, D, parts=1)
    brow = bF(RG_BROW, D, parts=1)
    gate_sb = bB(RG_GATE, DC * NE)
    t2Th = bB(RG_T2TH, DC * 512)
    lat = bB(RG_LAT, LATC * 512)
    moe = bB(RG_MOE, LATC * 512)
    hup = bB(RG_HUP, INTC * 512)
    stm = [bB(RG_STM + i * 8192, D) for i in range(4)]
    sch = [bB(RG_SCH0, 512), bB(RG_SCH1, 512)]
    logi = [bF(RG_LOGI0, NE), bF(RG_LOGI1, NE)]
    msk8 = bF(RG_MSK8, NE)
    rwrow = bF(RG_RWROW, 512, parts=1)
    rwb = bF(RG_RWB, 512)
    rsc = [bF(RG_RSC0, 512), bF(RG_RSC1, 512)]
    ftile = bF(RG_FT, D)
    trash = bF(RG_TRASH, D)
    x2pc = [bF(RG_X2PC0, 2048), bF(RG_X2PC1, 2048)]
    gpc = [bF(RG_GPC0, 2048), bF(RG_GPC1, 2048)]
    fc1_sl = [bB(RG_FC1SL + s * 8192, DC * 128) for s in range(2)]
    exp_sl = [bB(RG_EXPSL + s * 8192, 2 * LATC * 128) for s in range(2)]
    up_sl = [bB(RG_UPSL, DC * 128), bB(RG_UPSL + 8192, DC * 128), bB(RG_UPSL3, DC * 128)]
    dw_sl = [bB(RG_DWSL + s * 16384, INTC * 128) for s in range(2)]
    f2_sl = [bB(RG_F2SL + s * 8192, 2 * LATC * 128) for s in range(2)]

    for q in (sy, po):
        p.wait(q, "SS0", f_ss[0])
        p.wait(q, "SS1", f_ss[1])
    p.inc(sy.dma_start(grow, dgam), "SA", 16)
    p.inc(sy.dma_start(brow, dbet), "SA", 16)
    cg = p.inc(po.dma_start(gate_sb, dgate.rearrange("p a b -> p (a b)")), "WQ", 16)
    p.wait(te, "SA")
    bank_free = {i: ("SC", a_ready) for i in range(4)}
    for row, dst in ((grow, gb_g), (brow, gb_b)):
        for g in range(8):
            bk = g % 2
            sem, v = bank_free[bk]
            p.wait(te, sem, v)
            it = te.matmul(pbf[bk], ones_f32, row[:, g * 512 : (g + 1) * 512],
                           start=True, stop=True)
            c_mm = p.inc(it, "SP")
            p.wait(ve, "SP", c_mm)
            iv = ve.tensor_copy(dst[:, g * 512 : (g + 1) * 512], pbf[bk])
            bank_free[bk] = ("SV", p.inc(iv, "SV"))
    gb_ready = p.cnt["SV"]

    t2h3 = t2Th.rearrange("p (k c) -> p k c", k=DC)
    t2_loads = {}

    def load_t2(half):
        p.wait(sy, "SA")
        for tl in range(4):
            ins = sy.dma_start(t2h3[:, :, tl * 128 : (tl + 1) * 128],
                               dt2sp[half * 4 + tl])
            t2_loads[half] = p.inc(ins, "SA", 16)

    load_t2(0)
    p.wait(te, "SC")
    tb_free[0] = tb_free[1] = p.cnt["SV"]
    for half in range(2):
        if half == 1:
            p.wait(po, "SP", down_done)
            if dtaps:
                p.wait(po, "SA")
                p.wait(sy, "SA")

        p.wait(te, "SA", t2_loads[half])
        p.wait(te, "WQ", cg)
        last_rw_t = 0
        for tl in range(4):
            bk = tl % 2
            sem, v = bank_free[bk]
            p.wait(te, sem, v)
            for k in range(DC):
                it = te.matmul(pbf[bk][:, 0:NE],
                               t2Th[:, k * 512 + tl * 128 : k * 512 + (tl + 1) * 128],
                               gate_sb[:, k * NE : (k + 1) * NE],
                               start=(k == 0), stop=(k == DC - 1))
            c_mm = p.inc(it, "SP")
            par = tl % 2
            p.wait(ve, "SP", c_mm)
            iv = ve.tensor_copy(logi[par], pbf[bk][:, 0:NE])
            bank_free[bk] = ("SV", p.inc(iv, "SV"))
            ve.reduce_max(stat(S_M1), logi[par], axis=AX.X)
            ve.tensor_scalar(msk8, logi[par], stat(S_M1), None, ALU.is_ge)
            ve.scalar_tensor_tensor(msk8, msk8, -1e30, logi[par], ALU.mult, ALU.add)
            ve.reduce_max(stat(S_M2), msk8, axis=AX.X)
            iv = ve.tensor_tensor(stat(S_RW), stat(S_M1), stat(S_M2), ALU.add)
            c_rw = p.inc(iv, "SV")
            p.wait(te, "SV", c_rw)
            it = te.transpose(pbf[2][0:1, tl * 128 : (tl + 1) * 128], stat(S_RW), id_f32)
            last_rw_t = p.inc(it, "SP")
        p.wait(ve, "SP", last_rw_t)
        iv = ve.tensor_copy(rwrow, pbf[2][0:1, 0:512])
        c_row = p.inc(iv, "SV")
        p.wait(te, "SV", c_row)
        it = te.matmul(pbf[3], ones_f32, rwrow, start=True, stop=True)
        c_mm = p.inc(it, "SP")
        p.wait(ve, "SP", c_mm)
        rwb_ready = p.inc(ve.tensor_copy(rwb, pbf[3]), "SV")
        bank_free[2] = ("SV", rwb_ready)
        bank_free[3] = ("SV", rwb_ready)

        def p1(n_m, n_k, dram_fn, rhs_fn, evac_fn, slots, slot_init, dsem):
            L = len(slots)
            slot_last = list(slot_init)
            ready = {}

            def dma(m):
                s = m % L
                p.wait(po, "SP", slot_last[s])
                ready[m] = (dsem[s], p.inc(po.dma_start(slots[s], dram_fn(m)), dsem[s], 16))

            for m0 in range(min(L - 1, n_m)):
                dma(m0)
            last_mm = 0
            for m in range(n_m):
                s = m % L
                if m + L - 1 < n_m:
                    dma(m + L - 1)
                bk = m % 2
                sem, v = bank_free[bk]
                p.wait(te, sem, v)
                p.wait(te, *ready[m])
                for k in range(n_k):
                    it = te.matmul(pbf[bk], slots[s][:, k * 128 : (k + 1) * 128],
                                   rhs_fn(k), start=(k == 0), stop=(k == n_k - 1))
                c_mm = p.inc(it, "SP")
                slot_last[s] = c_mm
                last_mm = c_mm
                evac_fn(m, bk, c_mm)
            return last_mm

        fc1_first = [half == 1]

        def fc1_evac(m, bk, c_mm):
            p.wait(ve, "SP", c_mm)
            if fc1_first[0]:
                p.wait(ve, "SS0")
                p.wait(ve, "SS1")
                fc1_first[0] = False
            iv = ve.tensor_copy(lat[:, m * 512 : (m + 1) * 512], pbf[bk])
            bank_free[bk] = ("SV", p.inc(iv, "SV"))

        fc1_done = p1(LATC, DC, lambda m: dfc1[m].rearrange("p a b -> p (a b)"),
                      lambda k: t2Th[:, k * 512 : (k + 1) * 512],
                      fc1_evac, fc1_sl, [0, 0], ("D0", "D1"))

        def exp_evac(m, bk, c_mm):
            p.wait(ve, "SP", c_mm)
            p.wait(ve, "SV", rwb_ready)
            iv = ve.scalar_tensor_tensor(moe[:, m * 512 : (m + 1) * 512],
                                         pbf[bk], 1.0, rwb, ALU.mult, ALU.mult)
            bank_free[bk] = ("SV", p.inc(iv, "SV"))

        p.wait(te, "SV")
        exp_slot_last = [0, 0]
        exp_ready = {}

        def exp_dma(g):
            s = g % 2
            p.wait(po, "SP", exp_slot_last[s])
            ins = po.dma_start(
                exp_sl[s].rearrange("p (g c) -> p g c", g=2),
                dexp[2 * g : 2 * g + 2].rearrange("g p a b -> p g (a b)"))
            exp_ready[g] = (f"D{2 + s}", p.inc(ins, f"D{2 + s}", 16))

        exp_dma(0)
        exp_done = 0
        for g in range(PAIRS):
            s = g % 2
            if g + 1 < PAIRS:
                exp_dma(g + 1)
            for j in range(2):
                m = 2 * g + j
                bk = m % 2
                sem, v = bank_free[bk]
                p.wait(te, sem, v)
                p.wait(te, *exp_ready[g])
                for k in range(LATC):
                    it = te.matmul(
                        pbf[bk],
                        exp_sl[s][:, (j * LATC + k) * 128 : (j * LATC + k + 1) * 128],
                        lat[:, k * 512 : (k + 1) * 512],
                        start=(k == 0), stop=(k == LATC - 1))
                c_mm = p.inc(it, "SP")
                exp_done = c_mm
                exp_evac(m, bk, c_mm)
            exp_slot_last[s] = exp_done

        up_first = [half == 1]

        def up_evac(m, bk, c_mm):
            par = m % 2
            p.wait(se, "SP", c_mm)
            if up_first[0]:
                p.wait(se, "SV")
                up_first[0] = False
            se.activation(rsc[par], pbf[bk], AF.Relu)
            ia = se.activation(hup[:, m * 512 : (m + 1) * 512], rsc[par], AF.Square)
            bank_free[bk] = ("SC", p.inc(ia, "SC"))

        up_done = p1(INTC, DC, lambda m: dup[m].rearrange("p a b -> p (a b)"),
                     lambda k: t2Th[:, k * 512 : (k + 1) * 512],
                     up_evac, up_sl, [exp_done] * 3, ("D0", "D1", "D4"))
        hup_ready = p.cnt["SC"]
        p.wait(te, "SC", hup_ready)
        p.wait(te, "SV")

        f2_slot_last = [up_done, up_done]

        def down_dram(m):
            return ddown[m].rearrange("p a b -> p (a b)")

        def down_extra(m, bk):
            s = (m // 2) % 2
            j = m % 2
            for k in range(LATC):
                it = te.matmul(pbf[bk],
                               f2_sl[s][:, (j * LATC + k) * 128 : (j * LATC + k + 1) * 128],
                               moe[:, k * 512 : (k + 1) * 512],
                               start=False, stop=(k == LATC - 1))
            return it

        down_first = [half == 1]

        def down_evac(m, bk, c_mm):
            par = m % 2
            p.wait(ve, "SP", c_mm)
            if down_first[0]:
                p.wait(ve, "SC")
                down_first[0] = False
            iv = ve.tensor_copy(sch[par], pbf[bk])
            c_sch = p.inc(iv, "SV")
            bank_free[bk] = ("SV", c_sch)
            p.wait(te, "SV", c_sch)
            p.wait(te, "SV", tb_free[par])
            for j in range(4):
                it = te.transpose(pbb[4 + par][:, j * 128 : (j + 1) * 128],
                                  sch[par][:, j * 128 : (j + 1) * 128], id_bf)
            c_t = p.inc(it, "SP")
            p.wait(ve, "SP", c_t)
            for j in range(4):
                iv = ve.tensor_copy(stm[j][:, m * 128 : (m + 1) * 128],
                                    pbb[4 + par][:, j * 128 : (j + 1) * 128])
            tb_free[par] = p.inc(iv, "SV")

        slot_last_d = [up_done, up_done]
        slot_last_f = [up_done, up_done]
        down_ready = {}
        f2_ready = {}

        def down_dma(m):
            s = m % 2
            p.wait(po, "SP", slot_last_d[s])
            down_ready[m] = (f"D{s}", p.inc(po.dma_start(dw_sl[s], down_dram(m)), f"D{s}", 16))

        def f2_dma(g):
            s = g % 2
            p.wait(po, "SP", slot_last_f[s])
            ins = po.dma_start(
                f2_sl[s].rearrange("p (g c) -> p g c", g=2),
                dfc2[2 * g : 2 * g + 2].rearrange("g p a b -> p g (a b)"))
            f2_ready[g] = (f"D{2 + s}", p.inc(ins, f"D{2 + s}", 16))

        down_dma(0)
        f2_dma(0)
        down_done = 0
        for m in range(DC):
            s = m % 2
            if m + 1 < DC:
                down_dma(m + 1)
            if m % 2 == 0 and m // 2 + 1 < LATC:
                f2_dma(m // 2 + 1)
            bk = m % 2
            sem, v = bank_free[bk]
            p.wait(te, sem, v)
            p.wait(te, *down_ready[m])
            p.wait(te, *f2_ready[m // 2])
            for k in range(INTC):
                te.matmul(pbf[bk], dw_sl[s][:, k * 128 : (k + 1) * 128],
                          hup[:, k * 512 : (k + 1) * 512],
                          start=(k == 0), stop=False)
            it = down_extra(m, bk)
            c_mm = p.inc(it, "SP")
            slot_last_d[s] = c_mm
            if m % 2 == 1:
                slot_last_f[(m // 2) % 2] = c_mm
            down_done = c_mm
            down_evac(m, bk, c_mm)
        s_ready = p.cnt["SV"]
        if half == 0:
            p.wait(sy, "SP", down_done)
            load_t2(1)

        p.wait(ve, "SV", s_ready)
        p.wait(se, "SV", s_ready)
        p.wait(se, "SP", down_done)
        out_dma = [0, 0]
        first_pc = True
        for tl in range(4):
            t = half * 4 + tl
            c_r = ln_stats(stm[tl], trash)
            p.wait(ve, "SC", c_r)
            ve.tensor_scalar(ftile, stm[tl], stat(S_MU), stat(S_R),
                             ALU.subtract, ALU.mult)
            ve.tensor_tensor(ftile, ftile, gb_g, ALU.mult)
            iv = ve.tensor_tensor(ftile, ftile, gb_b, ALU.add)
            c_f = p.inc(iv, "SV")
            p.wait(se, "SV", c_f)
            c_r2 = ln_stats(ftile, trash)
            p.wait(ve, "SC", c_r2)
            for pc in range(2):
                if first_pc:
                    p.wait(sy, "SP", down_done)
                    first_pc = False
                c_x2 = p.inc(sy.dma_start(x2pc[pc], dx2sp[t][:, pc * 2048 : (pc + 1) * 2048]),
                             f"SX{pc}", 16)
                p.wait(ve, f"SS{pc}", out_dma[pc])
                ve.tensor_scalar(gpc[pc], ftile[:, pc * 2048 : (pc + 1) * 2048],
                                 stat(S_MU), stat(S_R), ALU.subtract, ALU.mult)
                p.wait(ve, f"SX{pc}", c_x2)
                iv = ve.tensor_tensor(gpc[pc], gpc[pc], x2pc[pc], ALU.add)
                c_o = p.inc(iv, "SV")
                p.wait(sy, "SV", c_o)
                ins = sy.dma_start(dout[t * 128 : (t + 1) * 128, pc * 2048 : (pc + 1) * 2048],
                                   gpc[pc])
                out_dma[pc] = p.inc(ins, f"SS{pc}", 16)
        if half == 0 and dtaps:
            p.wait(sy, "SA")
            for nm, buf, cols in (("lat", lat, LATC * 512), ("moe", moe, LATC * 512),
                                  ("hup", hup, INTC * 512)):
                if nm in dtaps:
                    p.wait(sy, "SV", p.cnt["SV"])
                    p.inc(sy.dma_start(dtaps[nm], buf), "SA", 16)
            if "stm" in dtaps:
                p.wait(sy, "SV", p.cnt["SV"])
                for j in range(4):
                    p.inc(sy.dma_start(dtaps["stm"][j * 128 : (j + 1) * 128, :], stm[j]),
                          "SA", 16)

    return finish()


def prep_weights(inp):
    w = {}
    cw = np.asarray(inp["L0_attn_conv_w"])
    w["wconv"] = np.ascontiguousarray(
        cw.reshape(3, DC, 128, W3C, 128).transpose(3, 0, 2, 1, 4))
    for nm, key in (("wq", "L0_attn_wq"), ("wk", "L0_attn_wk"), ("wv", "L0_attn_wv")):
        w[nm] = np.ascontiguousarray(np.asarray(inp[key]).reshape(WDC, 128, 1024))
    w["wo"] = np.ascontiguousarray(np.asarray(inp["L0_attn_wo"]).reshape(PAIRS, 128, 1024))
    w["projw"] = np.ascontiguousarray(
        np.asarray(inp["L0_attn_proj_w"]).reshape(WDC, 128, DC, 128).transpose(2, 1, 0, 3))
    w["gater"] = np.ascontiguousarray(
        np.asarray(inp["L0_ffn_gate_w"]).reshape(DC, 128, NE).transpose(1, 0, 2))
    w["upw"] = np.ascontiguousarray(
        np.asarray(inp["L0_ffn_up_w"]).reshape(DC, 128, INTC, 128).transpose(2, 1, 0, 3))
    w["downw"] = np.ascontiguousarray(
        np.asarray(inp["L0_ffn_down_w"]).reshape(INTC, 128, DC, 128).transpose(2, 1, 0, 3))
    w["fc1w"] = np.ascontiguousarray(
        np.asarray(inp["L0_ffn_fc1_w"]).reshape(DC, 128, LATC, 128).transpose(2, 1, 0, 3))
    w["expw"] = np.ascontiguousarray(
        np.asarray(inp["L0_ffn_expert_w"]).reshape(LATC, 128, LATC, 128).transpose(2, 1, 0, 3))
    w["fc2w"] = np.ascontiguousarray(
        np.asarray(inp["L0_ffn_fc2_w"]).reshape(LATC, 128, DC, 128).transpose(2, 1, 0, 3))
    w["convb"] = np.ascontiguousarray(np.asarray(inp["L0_attn_conv_b"]).reshape(W3C, 128))
    w["bq"] = np.ascontiguousarray(np.asarray(inp["L0_attn_bq"]).reshape(16, 64))
    w["bk"] = np.ascontiguousarray(np.asarray(inp["L0_attn_bk"]).reshape(16, 64))
    w["bv"] = np.ascontiguousarray(np.asarray(inp["L0_attn_bv"]).reshape(1, 1024))
    w["bo"] = np.ascontiguousarray(np.asarray(inp["L0_attn_bo"]).reshape(PAIRS, 128))
    w["projb"] = np.ascontiguousarray(np.asarray(inp["L0_attn_proj_b"]).reshape(DC, 128))
    w["gamma"] = np.ascontiguousarray(np.asarray(inp["L0_ffn_ln_gamma"]).reshape(1, D))
    w["beta"] = np.ascontiguousarray(np.asarray(inp["L0_ffn_ln_beta"]).reshape(1, D))
    import ml_dtypes
    w["id_bf"] = np.eye(128, dtype=ml_dtypes.bfloat16)
    w["id_f32"] = np.eye(128, dtype=np.float32)
    tril = np.tril(np.ones((128, 128), bool))
    w["mask"] = np.where(tril, np.float32(0.0), np.float32(NEG))
    return w


def core_in_map(inp, w, core):
    x_full = np.asarray(inp["L0_attn_inputs"])
    n, part = core // 4, core % 4
    s0 = part * 1024
    x = np.ascontiguousarray(x_full[n, s0 : s0 + 1024])
    halo = np.zeros((2, D), np.float32)
    if s0 > 0:
        halo[0] = x_full[n, s0 - 1]
    if s0 + T < 4096:
        halo[1] = x_full[n, s0 + T]
    m = {"x": x, "xhalo": halo}
    m.update(w)
    return m


def kernel(**inputs):
    w = prep_weights(inputs)
    nc = build()
    in_maps = [core_in_map(inputs, w, c) for c in range(8)]
    res = run_bass_kernel_spmd(nc, in_maps, core_ids=list(range(8)))
    out = np.concatenate([res.results[c]["out"] for c in range(8)], axis=0)
    return out.reshape(2, 4096, D)
